# version 04
# baseline (speedup 1.0000x reference)
import numpy as np
import ml_dtypes
from contextlib import ExitStack

import concourse.bass as bass
import concourse.mybir as mybir
from concourse.bass_utils import run_bass_kernel_spmd

F32 = mybir.dt.float32
BF16 = mybir.dt.bfloat16
AF = mybir.ActivationFunctionType
ALU = mybir.AluOpType
AX = mybir.AxisListType

P = 128
S = 2048
E = 2048
NT = S // P
NCH = E // P
EPS = 1e-5
QKSCALE = float(E) ** (-0.25)
LNQK = float(np.log(QKSCALE))

ENGINES = ("sync", "scalar", "vector", "tensor", "gpsimd")
NONE_TOK = (None, 0)


class Sem:
    def __init__(self, ctx, nc, name, registry):
        self.h = ctx.enter_context(nc.semaphore(name))
        self.n = 0
        self.name = name
        registry.append(self)


class Sched:
    def __init__(self):
        self.q = {e: [] for e in ENGINES}
        self.last_wait = {e: {} for e in ENGINES}

    def op(self, eng, build, waits=(), inc=None, k=1):
        wl = []
        lw = self.last_wait[eng]
        for (s, v) in waits:
            if s is None or v <= 0:
                continue
            if lw.get(s.name, -1) >= v:
                continue
            lw[s.name] = v
            wl.append((s, v))
        post = None
        if inc is not None:
            inc.n += k
            post = inc.n

        def emit(engine, wl=wl, build=build, inc=inc, k=k):
            for (s, v) in wl:
                engine.wait_ge(s.h, v)
            ins = build(engine)
            if inc is not None:
                ins.then_inc(inc.h, k)

        self.q[eng].append(emit)
        return (inc, post)

    def wait(self, eng, *tokens):
        wl = []
        lw = self.last_wait[eng]
        for (s, v) in tokens:
            if s is None or v <= 0 or lw.get(s.name, -1) >= v:
                continue
            lw[s.name] = v
            wl.append((s, v))
        if not wl:
            return

        def emit(engine, wl=wl):
            for (s, v) in wl:
                engine.wait_ge(s.h, v)

        self.q[eng].append(emit)

    def replay(self, block):
        for eng in ENGINES:
            thunks = self.q[eng]
            if not thunks:
                continue

            def run(engine, thunks=thunks):
                for t in thunks:
                    t(engine)

            getattr(block, eng)(run)


def build_nc(debug=False):
    nc = bass.Bass("TRN2", target_bir_lowering=False, debug=False,
                   dynamic_dma_scratch_size=16384)

    x_d = nc.dram_tensor("x", [S, E], F32, kind="ExternalInput").ap()
    wqkvT_d = nc.dram_tensor("wqkvT", [E, 3 * E], F32, kind="ExternalInput").ap()
    woutT_d = nc.dram_tensor("woutT", [E, E], F32, kind="ExternalInput").ap()
    wffnT_d = nc.dram_tensor("wffnT", [E, E], F32, kind="ExternalInput").ap()
    bqkv_d = nc.dram_tensor("bqkv_c", [P, 3 * NCH], F32, kind="ExternalInput").ap()
    bout_d = nc.dram_tensor("bout_c", [P, NCH], F32, kind="ExternalInput").ap()
    gamma_d = nc.dram_tensor("gamma_c", [P, NCH], F32, kind="ExternalInput").ap()
    beta_d = nc.dram_tensor("beta_c", [P, NCH], F32, kind="ExternalInput").ap()
    lnw_c_d = nc.dram_tensor("lnw_c", [P, NCH], F32, kind="ExternalInput").ap()
    lnb_c_d = nc.dram_tensor("lnb_c", [P, NCH], F32, kind="ExternalInput").ap()
    cos_d = nc.dram_tensor("cos_c", [P, NCH], F32, kind="ExternalInput").ap()
    sin_d = nc.dram_tensor("sin_c", [P, NCH], F32, kind="ExternalInput").ap()
    ident_d = nc.dram_tensor("identb", [P, P], BF16, kind="ExternalInput").ap()
    maskT_d = nc.dram_tensor("maskTb", [P, P], BF16, kind="ExternalInput").ap()

    out_d = nc.dram_tensor("out", [S, E], F32, kind="ExternalOutput").ap()

    skind = "ExternalOutput" if debug else "Internal"
    xnT_sp = nc.dram_tensor("xnT_sp", [NT, P, E], BF16, kind=skind).ap()
    xn2T_sp = nc.dram_tensor("xn2T_sp", [NT, P, E], BF16, kind=skind).ap()
    qT_sp = nc.dram_tensor("qT_sp", [NT, P, E], BF16, kind=skind).ap()
    kT_sp = nc.dram_tensor("kT_sp", [NT, P, E], BF16, kind=skind).ap()
    attnT_sp = nc.dram_tensor("attnT_sp", [NT, P, E], BF16, kind=skind).ap()
    dbg = {}
    if debug:
        dbg["sg_sb"] = nc.dram_tensor("d_sg", [P, NCH * E], BF16,
                                      kind="ExternalOutput").ap()
        dbg["s"] = nc.dram_tensor("d_s", [S, E], F32, kind="ExternalOutput").ap()
        dbg["at"] = nc.dram_tensor("d_at", [NT, P, E], BF16, kind="ExternalOutput").ap()
        dbg["wo_sb"] = nc.dram_tensor("d_wo", [P, NCH * E], BF16, kind="ExternalOutput").ap()
        dbg["bout_b"] = nc.dram_tensor("d_bout", [P, E], F32, kind="ExternalOutput").ap()
        dbg["gamS_b"] = nc.dram_tensor("d_gam", [P, E], F32, kind="ExternalOutput").ap()
        dbg["beta_b"] = nc.dram_tensor("d_beta", [P, E], F32, kind="ExternalOutput").ap()
        dbg["f"] = nc.dram_tensor("d_f", [S, E], F32, kind="ExternalOutput").ap()
        dbg["g"] = nc.dram_tensor("d_g", [S, E], F32, kind="ExternalOutput").ap()

    Sh = Sched()
    wq_view = wqkvT_d.rearrange("(c p) f -> c p f", p=P)

    with ExitStack() as ctx:
        SEMS = []

        def sem(name):
            return Sem(ctx, nc, name, SEMS)

        def sem2(name):
            return [sem(name + "0"), sem(name + "1")]

        PEm = sem("s_pe")
        VE = sem("s_ve")
        ACT = sem("s_ac")
        CONSTS = sem("s_consts")
        WGRP = [sem(f"s_wg{i}") for i in range(6)]
        WGRPB = [sem(f"s_wgb{i}") for i in range(6)]
        WO4 = [sem(f"s_wo4_{i}") for i in range(4)]
        SG4 = [sem(f"s_sg4_{i}") for i in range(4)]

        def wsem(g, c):
            return WGRP[g] if c < NCH // 2 else WGRPB[g]
        SX = sem2("s_x")
        SPX = sem2("s_spx")
        SP2 = sem2("s_sp2")
        SPQ = sem2("s_spq")
        SPK = sem2("s_spk")
        SPA = sem2("s_spa")
        SQX = sem2("s_qx")
        SQT = sem2("s_qt")
        SAT = sem2("s_at")
        SX2 = sem2("s_x2")
        SXB = sem("s_xb")
        SWC = sem2("s_wc")
        OUTS = sem("s_outs")
        SDS = sem("s_ds")
        SDF = sem("s_df")
        SDG = sem("s_dg")

        def dma(eng, semm, out_ap, in_ap, waits=()):
            return Sh.op(eng, lambda e, o=out_ap, i=in_ap: e.dma_start(out=o, in_=i),
                         waits, inc=semm, k=16)

        def phase_guard():
            return [(s, s.n) for s in SEMS]

        identb = ctx.enter_context(nc.sbuf_tensor("sb_identb", [P, P], BF16))
        maskTb = ctx.enter_context(nc.sbuf_tensor("sb_maskTb", [P, P], BF16))
        onescol = ctx.enter_context(nc.sbuf_tensor("onescol", [P, 1], BF16))
        onesrow = ctx.enter_context(nc.sbuf_tensor("onesrow", [1, P], F32))
        epsb = ctx.enter_context(nc.sbuf_tensor("epsb", [P, 1], F32))
        eps24b = ctx.enter_context(nc.sbuf_tensor("eps24b", [P, 1], F32))
        lqkb = ctx.enter_context(nc.sbuf_tensor("lqkb", [P, 1], F32))
        wmax = ctx.enter_context(nc.sbuf_tensor("sb_wmax", [P, 1], F32))
        lnw_c = ctx.enter_context(nc.sbuf_tensor("sb_lnw_c", [P, NCH], F32))
        lnb_c = ctx.enter_context(nc.sbuf_tensor("sb_lnb_c", [P, NCH], F32))
        bqkv_c = ctx.enter_context(nc.sbuf_tensor("sb_bqkv_c", [P, 3 * NCH], F32))
        bout_c = ctx.enter_context(nc.sbuf_tensor("sb_bout_c", [P, NCH], F32))
        gamma_c = ctx.enter_context(nc.sbuf_tensor("sb_gamma_c", [P, NCH], F32))
        beta_c = ctx.enter_context(nc.sbuf_tensor("sb_beta_c", [P, NCH], F32))
        cos_c = ctx.enter_context(nc.sbuf_tensor("sb_cos_c", [P, NCH], F32))
        sin_c = ctx.enter_context(nc.sbuf_tensor("sb_sin_c", [P, NCH], F32))
        identf = ctx.enter_context(nc.sbuf_tensor("sb_identf", [P, P], F32))

        for dst, src in [(identb, ident_d), (maskTb, maskT_d), (lnw_c, lnw_c_d),
                         (lnb_c, lnb_c_d), (bqkv_c, bqkv_d), (bout_c, bout_d),
                         (gamma_c, gamma_d), (beta_c, beta_d), (cos_c, cos_d),
                         (sin_c, sin_d)]:
            dma("sync", CONSTS, dst[:], src)
        T_CONST = (CONSTS, CONSTS.n)

        t_ones = Sh.op("vector", lambda e: e.memset(onescol[:], 1.0), (), inc=VE)
        t_onesr = Sh.op("vector", lambda e: e.memset(onesrow[:], 1.0), (), inc=VE)
        t_eps = Sh.op("vector", lambda e: e.memset(epsb[:], EPS), (), inc=VE)
        Sh.op("vector", lambda e: e.memset(eps24b[:], 1e-24), (), inc=None)
        Sh.op("vector", lambda e: e.memset(lqkb[:], LNQK), (), inc=None)
        t_idf = Sh.op("vector", lambda e: e.tensor_copy(out=identf[:], in_=identb[:]),
                      [T_CONST], inc=VE)

        bc_state = {"free": NONE_TOK}

        def build_bcast(dst, cols_ap, c0, bc_ps, nch=NCH, extra_waits=()):
            last = None
            for c4 in range(nch // 4):
                tpe = None
                for j in range(4):
                    c = c4 * 4 + j
                    lhs = cols_ap[:, c0 + c:c0 + c + 1].broadcast_to([P, P])
                    w = [] if j else (list(extra_waits) +
                                      [bc_state["free"], (VE, t_idf[1]), T_CONST])
                    tpe = Sh.op("tensor",
                                lambda e, lhs=lhs, j=j, bc_ps=bc_ps:
                                e.matmul(bc_ps[:, j * 128:(j + 1) * 128],
                                         lhs, identf[:], start=True, stop=True),
                                w, inc=PEm if j == 3 else None)
                tve = Sh.op("vector",
                            lambda e, c4=c4, dst=dst, bc_ps=bc_ps:
                            e.tensor_copy(out=dst[:, c4 * 512:(c4 + 1) * 512],
                                          in_=bc_ps[:, 0:512]),
                            [(PEm, tpe[1])], inc=VE)
                bc_state["free"] = (VE, tve[1])
                last = tve
            return last

        def transpose16(tr_ps, tr_state, src, src_toks, evac_fn):
            last_ev = None
            last_pe = None
            for half in range(2):
                tps_last = None
                for j in range(8):
                    c = half * 8 + j
                    w = []
                    if j == 0:
                        w = list(src_toks) + [tr_state["free"][half], T_CONST]
                    tps_last = Sh.op("tensor", lambda e, c=c, j=j, half=half,
                                     src=src, tr_ps=tr_ps:
                                     e.transpose(
                                         tr_ps[:, half * 1024 + j * 128:
                                               half * 1024 + (j + 1) * 128],
                                         src[:, c * 128:(c + 1) * 128],
                                         identb[:]),
                                     w, inc=PEm if j == 7 else None)
                tev = evac_fn(half, (PEm, tps_last[1]))
                tr_state["free"][half] = tev
                last_ev = tev
                last_pe = tps_last
            return last_ev, (PEm, last_pe[1])

        t_xnT_sp = [None] * NT
        t_xn2T_sp = [None] * NT
        t_attnT_sp = [None] * NT
        qk_spill = {"q": [None] * NT, "k": [None] * NT}

        vsc = ExitStack()
        v_all = vsc.enter_context(nc.sbuf_tensor("v_all", [P, NT * E], BF16))

        with ExitStack() as p1:
            e1 = p1.enter_context
            xbuf = [e1(nc.sbuf_tensor(f"xbuf{i}", [P, E], F32)) for i in range(2)]
            bn24 = [e1(nc.sbuf_tensor(f"bn24_{i}", [P, 48], F32)) for i in range(2)]
            mv1 = [e1(nc.sbuf_tensor(f"mv1_{i}", [P, 2], F32)) for i in range(2)]
            mv2 = [e1(nc.sbuf_tensor(f"mv2_{i}", [P, 2], F32)) for i in range(2)]
            rstd1 = [e1(nc.sbuf_tensor(f"rstd1_{i}", [P, 1], F32)) for i in range(2)]
            rstd2 = [e1(nc.sbuf_tensor(f"rstd2_{i}", [P, 1], F32)) for i in range(2)]
            negm1 = [e1(nc.sbuf_tensor(f"negm1_{i}", [P, 1], F32)) for i in range(2)]
            negm2 = [e1(nc.sbuf_tensor(f"negm2_{i}", [P, 1], F32)) for i in range(2)]
            xn = [e1(nc.sbuf_tensor(f"xn{i}", [P, E], F32)) for i in range(2)]
            xnb = [e1(nc.sbuf_tensor(f"xnb{i}", [P, E], BF16)) for i in range(2)]
            xn2b = [e1(nc.sbuf_tensor(f"xn2b{i}", [P, E], BF16)) for i in range(2)]
            xnT_t = [e1(nc.sbuf_tensor(f"xnT_t{i}", [P, E], BF16)) for i in range(2)]
            xn2T_t = [e1(nc.sbuf_tensor(f"xn2T_t{i}", [P, E], BF16)) for i in range(2)]
            wv_sb = e1(nc.sbuf_tensor("wv_sb", [P, NCH * E], BF16))
            bv_b = e1(nc.sbuf_tensor("bv_b", [P, E], F32))
            tr_ps = e1(nc.psum_tensor("tr_ps", [P, 2048], BF16))
            v_ps = e1(nc.psum_tensor("v_ps", [P, E], F32))
            bc_ps = e1(nc.psum_tensor("bc_ps", [P, 512], F32))

            for c in range(NCH):
                dma("gpsimd", wsem(0, c), wv_sb[:, c * E:(c + 1) * E],
                    wq_view[c, :, 4096:6144])
            T_WV = [(WGRP[0], WGRP[0].n), (WGRPB[0], WGRPB[0].n)]
            t_bvb = build_bcast(bv_b, bqkv_c[:], 2 * NCH, bc_ps)

            tr_state = {"free": [NONE_TOK, NONE_TOK]}
            xbuf_free = [NONE_TOK, NONE_TOK]
            xnT_free = [NONE_TOK, NONE_TOK]
            xn2T_free = [NONE_TOK, NONE_TOK]
            vps_free = NONE_TOK
            mv1_g = [NONE_TOK, NONE_TOK]
            mv2_g = [NONE_TOK, NONE_TOK]
            rstd_g = [NONE_TOK, NONE_TOK]
            prev_pe_xnT = [NONE_TOK, NONE_TOK]
            prev_pe_xn2T = [NONE_TOK, NONE_TOK]

            pend_vevac = []

            def flush_vevac():
                nonlocal vps_free
                while pend_vevac:
                    Tv, t_vmm = pend_vevac.pop(0)
                    evs_last = None
                    for fc in range(4):
                        evs_last = Sh.op("vector", lambda e, fc=fc, Tv=Tv:
                                         e.tensor_tensor(
                                             out=v_all[:, Tv * E + fc * 512:
                                                       Tv * E + (fc + 1) * 512],
                                             in0=v_ps[:, fc * 512:(fc + 1) * 512],
                                             in1=bv_b[:, fc * 512:(fc + 1) * 512],
                                             op=ALU.add),
                                         [t_vmm, (VE, t_bvb[1])], inc=VE)
                    vps_free = (VE, evs_last[1])

            for T in range(NT):
                bi = T % 2
                t_x = dma("sync", SX[bi], xbuf[bi][:], x_d[T * P:(T + 1) * P, :],
                          waits=[xbuf_free[bi]])
                TX = (SX[bi], t_x[1])
                t_bn = None
                for q4 in range(4):
                    t_bn = Sh.op("vector", lambda e, b=bi, q4=q4: e.bn_stats(
                        bn24[b][:, q4 * 6:(q4 + 1) * 6],
                        xbuf[b][:, q4 * 512:(q4 + 1) * 512]),
                        [TX], inc=VE if q4 == 3 else None)
                t_ag = Sh.op("vector", lambda e, b=bi: e.bn_aggr(mv1[b][:],
                                                                 bn24[b][:, 0:24]),
                             [(VE, t_bn[1]), mv1_g[bi]], inc=VE)
                t_nm = Sh.op("vector", lambda e, b=bi: e.tensor_scalar(
                    out=negm1[b][:], in0=mv1[b][:, 0:1], scalar1=-1.0, scalar2=None,
                    op0=ALU.mult), [(VE, t_ag[1])], inc=VE)
                t_l1 = Sh.op("scalar", lambda e, b=bi: e.activation(
                    out=rstd1[b][:], in_=mv1[b][:, 1:2], func=AF.Ln, bias=epsb[:]),
                    [(VE, t_ag[1]), (VE, t_eps[1]), rstd_g[bi]], inc=ACT)
                t_e1 = Sh.op("scalar", lambda e, b=bi: e.activation(
                    out=rstd1[b][:], in_=rstd1[b][:], func=AF.Exp, scale=-0.5),
                    [(ACT, t_l1[1])], inc=ACT)
                mv1_g[bi] = (ACT, t_l1[1])
                t_xn = Sh.op("vector", lambda e, b=bi: e.tensor_scalar(
                    out=xn[b][:], in0=xbuf[b][:], scalar1=negm1[b][:],
                    scalar2=rstd1[b][:], op0=ALU.add, op1=ALU.mult),
                    [(ACT, t_e1[1]), (VE, t_nm[1])], inc=VE)
                xbuf_free[bi] = (VE, t_xn[1])
                t_xnb = Sh.op("vector", lambda e, b=bi: e.tensor_copy(
                    out=xnb[b][:], in_=xn[b][:]),
                    [prev_pe_xnT[bi]], inc=VE)
                t_bn2 = None
                for q4 in range(4):
                    t_bn2 = Sh.op("vector", lambda e, b=bi, q4=q4: e.bn_stats(
                        bn24[b][:, 24 + q4 * 6:24 + (q4 + 1) * 6],
                        xn[b][:, q4 * 512:(q4 + 1) * 512]),
                        [], inc=VE if q4 == 3 else None)
                t_ag2 = Sh.op("vector", lambda e, b=bi: e.bn_aggr(
                    mv2[b][:], bn24[b][:, 24:48]),
                    [(VE, t_bn2[1]), mv2_g[bi]], inc=VE)
                t_nm2 = Sh.op("vector", lambda e, b=bi: e.tensor_scalar(
                    out=negm2[b][:], in0=mv2[b][:, 0:1], scalar1=-1.0, scalar2=None,
                    op0=ALU.mult), [(VE, t_ag2[1])], inc=VE)
                t_l2 = Sh.op("scalar", lambda e, b=bi: e.activation(
                    out=rstd2[b][:], in_=mv2[b][:, 1:2], func=AF.Ln, bias=epsb[:]),
                    [(VE, t_ag2[1])], inc=ACT)
                t_e2 = Sh.op("scalar", lambda e, b=bi: e.activation(
                    out=rstd2[b][:], in_=rstd2[b][:], func=AF.Exp, scale=-0.5),
                    [(ACT, t_l2[1])], inc=ACT)
                mv2_g[bi] = (ACT, t_l2[1])
                t_x2b = Sh.op("vector", lambda e, b=bi: e.tensor_scalar(
                    out=xn2b[b][:], in0=xn[b][:], scalar1=negm2[b][:],
                    scalar2=rstd2[b][:], op0=ALU.add, op1=ALU.mult),
                    [(ACT, t_e2[1]), (VE, t_nm2[1]), prev_pe_xn2T[bi]], inc=VE)
                rstd_g[bi] = (VE, t_x2b[1])
                flush_vevac()

                def ev_xnT(half, pe_tok, b=bi):
                    return Sh.op("scalar", lambda e, half=half, b=b:
                                 e.copy(
                                     out=xnT_t[b][:, half * 1024:(half + 1) * 1024],
                                     in_=tr_ps[:, half * 1024:(half + 1) * 1024]),
                                 [pe_tok, xnT_free[b]], inc=ACT)

                t_xnT_ev, pe_tok = transpose16(
                    tr_ps, tr_state, xnb[bi][:], [(VE, t_xnb[1])], ev_xnT)
                prev_pe_xnT[bi] = pe_tok
                tsp = dma("scalar", SPX[bi], xnT_sp[T], xnT_t[bi][:],
                          waits=[(ACT, t_xnT_ev[1])])
                t_xnT_sp[T] = tsp
                xnT_free[bi] = (SPX[bi], tsp[1])

                mm = None
                for ec in range(NCH):
                    w = []
                    if ec == 0:
                        w = [vps_free, (ACT, t_xnT_ev[1]), T_WV[0]]
                    elif ec == NCH // 2:
                        w = [T_WV[1]]
                    for fc in range(4):
                        mm = Sh.op("tensor", lambda e, ec=ec, fc=fc, b=bi:
                                   e.matmul(v_ps[:, fc * 512:(fc + 1) * 512],
                                            xnT_t[b][:, ec * 128:(ec + 1) * 128],
                                            wv_sb[:, ec * E + fc * 512:
                                                  ec * E + (fc + 1) * 512],
                                            start=(ec == 0), stop=(ec == NCH - 1)),
                                   w if fc == 0 else [],
                                   inc=PEm if ec == NCH - 1 else None)
                pend_vevac.append((T, (PEm, mm[1])))

                def ev_xn2T(half, pe_tok, b=bi):
                    last = None
                    for j in range(8):
                        c = half * 8 + j
                        w = [pe_tok, xn2T_free[b], T_CONST]
                        last = Sh.op("scalar", lambda e, c=c, j=j, half=half, b=b:
                                     e.activation(
                                         out=xn2T_t[b][:, c * 128:(c + 1) * 128],
                                         in_=tr_ps[:, half * 1024 + j * 128:
                                                   half * 1024 + (j + 1) * 128],
                                         func=AF.Identity,
                                         scale=lnw_c[:, c:c + 1],
                                         bias=lnb_c[:, c:c + 1]),
                                     w, inc=ACT if j == 7 else None)
                    return last

                t_xn2T_ev, pe_tok = transpose16(
                    tr_ps, tr_state, xn2b[bi][:], [(VE, t_x2b[1])], ev_xn2T)
                prev_pe_xn2T[bi] = pe_tok
                tsp = dma("scalar", SP2[bi], xn2T_sp[T], xn2T_t[bi][:],
                          waits=[(ACT, t_xn2T_ev[1])])
                t_xn2T_sp[T] = tsp
                xn2T_free[bi] = (SP2[bi], tsp[1])

            flush_vevac()

        def qk_phase(which):
            col0 = 0 if which == "q" else E
            sp = qT_sp if which == "q" else kT_sp
            spsem = SPQ if which == "q" else SPK
            with ExitStack() as pp:
                ee = pp.enter_context
                xnt_s = [ee(nc.sbuf_tensor(f"{which}_xnt{i}", [P, E], BF16))
                         for i in range(2)]
                w_sb = ee(nc.sbuf_tensor(f"{which}_w_sb", [P, NCH * E], BF16))
                cos_b = ee(nc.sbuf_tensor(f"{which}_cos_b", [P, E], F32))
                sin_b = ee(nc.sbuf_tensor(f"{which}_sin_b", [P, E], F32))
                b_b = ee(nc.sbuf_tensor(f"{which}_b_b", [P, E], F32))
                qf_t = ee(nc.sbuf_tensor(f"{which}_qf", [P, E], F32))
                qf = [qf_t, qf_t]
                qa_t = ee(nc.sbuf_tensor(f"{which}_qa", [P, E], F32))
                qa = [qa_t, qa_t]
                qb_t = ee(nc.sbuf_tensor(f"{which}_qb", [P, E], F32))
                qb = [qb_t, qb_t]
                qbf = [ee(nc.sbuf_tensor(f"{which}_qbf{i}", [P, E], BF16))
                       for i in range(2)]
                ss = [ee(nc.sbuf_tensor(f"{which}_ss{i}", [P, 1], F32))
                      for i in range(2)]
                qsc = [ee(nc.sbuf_tensor(f"{which}_qsc{i}", [P, 1], F32))
                       for i in range(2)]
                qT_t = [ee(nc.sbuf_tensor(f"{which}_qT{i}", [P, E], BF16))
                        for i in range(2)]
                q_ps = ee(nc.psum_tensor(f"{which}_q_ps", [P, E], F32))
                tr_ps = ee(nc.psum_tensor(f"{which}_tr_ps", [P, 2048], BF16))
                bc_ps = ee(nc.psum_tensor(f"{which}_bc_ps", [P, 512], F32))

                guard = phase_guard()
                gidx = 1 if which == "q" else 2
                for c in range(NCH):
                    dma("gpsimd", wsem(gidx, c), w_sb[:, c * E:(c + 1) * E],
                        wq_view[c, :, col0:col0 + E],
                        waits=guard if c == 0 else ())
                T_W = [(WGRP[gidx], WGRP[gidx].n), (WGRPB[gidx], WGRPB[gidx].n)]
                bc0 = 0 if which == "q" else NCH
                t_bb = build_bcast(b_b, bqkv_c[:], bc0, bc_ps, extra_waits=guard)
                t_cb = build_bcast(cos_b, cos_c[:], 0, bc_ps)
                t_sb = build_bcast(sin_b, sin_c[:], 0, bc_ps)
                T_SPX_ALL = [(SPX[0], SPX[0].n), (SPX[1], SPX[1].n)]

                tr_state = {"free": [NONE_TOK, NONE_TOK]}
                xnt_free = [NONE_TOK, NONE_TOK]
                qT_free = [NONE_TOK, NONE_TOK]
                qps_free = NONE_TOK
                ssq_g = [NONE_TOK, NONE_TOK]
                qbf_tr = [NONE_TOK, NONE_TOK]

                def emit_transposes(T):
                    bi = T % 2

                    def ev_qT(half, pe_tok, b=bi):
                        return Sh.op("vector", lambda e, half=half, b=b:
                                     e.tensor_copy(
                                         out=qT_t[b][:, half * 1024:(half + 1) * 1024],
                                         in_=tr_ps[:, half * 1024:(half + 1) * 1024]),
                                     [pe_tok, qT_free[b]], inc=VE)

                    t_ev, pe_tok = transpose16(tr_ps, tr_state, qbf[bi][:],
                                               [(VE, chain_tok[T])], ev_qT)
                    qbf_tr[bi] = pe_tok
                    tsp = dma("scalar", spsem[bi], sp[T], qT_t[bi][:],
                              waits=[(VE, t_ev[1])])
                    qk_spill[which][T] = tsp
                    qT_free[bi] = (spsem[bi], tsp[1])

                chain_tok = {}
                for T in range(NT):
                    bi = T % 2
                    t_x = dma("sync", SQX[bi], xnt_s[bi][:], xnT_sp[T],
                              waits=[xnt_free[bi]] + T_SPX_ALL)
                    TX = (SQX[bi], t_x[1])
                    mm = None
                    for ec in range(NCH):
                        w = [TX]
                        if ec == 0:
                            w += [qps_free, T_W[0]]
                        elif ec == NCH // 2:
                            w += [T_W[1]]
                        for fc in range(4):
                            mm = Sh.op("tensor", lambda e, ec=ec, fc=fc, b=bi:
                                       e.matmul(q_ps[:, fc * 512:(fc + 1) * 512],
                                                xnt_s[b][:, ec * 128:(ec + 1) * 128],
                                                w_sb[:, ec * E + fc * 512:
                                                     ec * E + (fc + 1) * 512],
                                                start=(ec == 0),
                                                stop=(ec == NCH - 1)),
                                       w if fc == 0 else [],
                                       inc=PEm if ec == NCH - 1 else None)
                    t_mm = (PEm, mm[1])
                    xnt_free[bi] = t_mm
                    if T > 0:
                        emit_transposes(T - 1)
                    evs_last = None
                    for fc in range(4):
                        evs_last = Sh.op("vector", lambda e, fc=fc, b=bi:
                                         e.tensor_tensor(
                                             out=qf[b][:, fc * 512:(fc + 1) * 512],
                                             in0=q_ps[:, fc * 512:(fc + 1) * 512],
                                             in1=b_b[:, fc * 512:(fc + 1) * 512],
                                             op=ALU.add),
                                         [t_mm, (VE, t_bb[1]),
                                          ssq_g[bi] if fc == 0 else NONE_TOK],
                                         inc=VE)
                    qps_free = (VE, evs_last[1])
                    t_a = Sh.op("vector", lambda e, b=bi: e.tensor_tensor(
                        out=qa[b][:], in0=qf[b][:], in1=cos_b[:], op=ALU.mult),
                        [(VE, t_sb[1]), ssq_g[0], ssq_g[1]], inc=VE)
                    t_b1 = Sh.op("vector", lambda e, b=bi: e.tensor_tensor(
                        out=qb[b][:, 0:1024], in0=qf[b][:, 1:E:2],
                        in1=sin_b[:, 0:1024], op=ALU.mult),
                        [ssq_g[0], ssq_g[1]], inc=VE)
                    t_b2 = Sh.op("vector", lambda e, b=bi: e.tensor_tensor(
                        out=qb[b][:, 1024:E], in0=qf[b][:, 0:E:2],
                        in1=sin_b[:, 1024:E], op=ALU.mult), [], inc=VE)
                    t_c1 = Sh.op("vector", lambda e, b=bi: e.tensor_tensor(
                        out=qa[b][:, 0:1024], in0=qa[b][:, 0:1024],
                        in1=qb[b][:, 0:1024], op=ALU.subtract),
                        [(VE, t_b1[1])], inc=VE)
                    t_c2 = Sh.op("vector", lambda e, b=bi: e.tensor_tensor(
                        out=qa[b][:, 1024:E], in0=qa[b][:, 1024:E],
                        in1=qb[b][:, 1024:E], op=ALU.add),
                        [(VE, t_b2[1])], inc=VE)
                    t_ssq = Sh.op("scalar", lambda e, b=bi: e.activation(
                        out=qb[b][:], in_=qa[b][:], func=AF.Square,
                        accum_out=ss[b][:]),
                        [(VE, t_c1[1]), (VE, t_c2[1])], inc=ACT)
                    ssq_g[bi] = (ACT, t_ssq[1])
                    t_l = Sh.op("scalar", lambda e, b=bi: e.activation(
                        out=ss[b][:], in_=ss[b][:], func=AF.Ln, bias=eps24b[:]),
                        [(ACT, t_ssq[1])], inc=ACT)
                    t_e = Sh.op("scalar", lambda e, b=bi: e.activation(
                        out=qsc[b][:], in_=ss[b][:], func=AF.Exp, scale=-0.5,
                        bias=lqkb[:]), [(ACT, t_l[1])], inc=ACT)
                    t_qbf = Sh.op("vector", lambda e, b=bi: e.tensor_scalar(
                        out=qbf[b][:], in0=qa[b][:], scalar1=qsc[b][:],
                        scalar2=None, op0=ALU.mult),
                        [(ACT, t_e[1]), qbf_tr[bi]], inc=VE)
                    chain_tok[T] = t_qbf[1]
                emit_transposes(NT - 1)

        qk_phase("q")
        qk_phase("k")

        with ExitStack() as p2:
            e2 = p2.enter_context
            kT_all = e2(nc.sbuf_tensor("kT_all", [P, NT * E], BF16))
            qt_s = [e2(nc.sbuf_tensor(f"qt_s{i}", [P, E], BF16)) for i in range(2)]
            expT = [e2(nc.sbuf_tensor(f"expT{i}", [P, 1024], BF16)) for i in range(2)]
            rinv = e2(nc.sbuf_tensor("rinv", [1, 1024], F32))
            rinvb = e2(nc.sbuf_tensor("rinvb", [P, 1024], F32))
            attnT_t = [e2(nc.sbuf_tensor(f"attnT_t{i}", [P, E], BF16))
                       for i in range(2)]
            wchunk = [e2(nc.sbuf_tensor(f"wchunk{i}", [P, E], F32))
                      for i in range(2)]
            cmax = e2(nc.sbuf_tensor("cmax", [P, 1], F32))
            sc_ps = [e2(nc.psum_tensor(f"sc_ps{i}", [P, 512], F32)) for i in range(2)]
            at_ps = e2(nc.psum_tensor("at_ps", [P, 1024], F32))
            rs_ps = e2(nc.psum_tensor("rs_ps", [1, 1024], F32))
            rb_ps = e2(nc.psum_tensor("rb_ps", [P, 1024], F32))

            guard = phase_guard()
            kt_view = kT_sp.rearrange("T p c -> p T c")
            KT4 = [WGRP[4], WGRPB[4], WO4[0], SG4[0]]
            T_KT4 = []
            for k4 in range(4):
                t_kt = Sh.op("sync", lambda e, k4=k4: e.dma_start(
                    out=kT_all[:].rearrange("p (T c) -> p T c", c=E)[:, k4 * 4:(k4 + 1) * 4, :],
                    in_=kt_view[:, k4 * 4:(k4 + 1) * 4, :]),
                    guard + [(SPK[0], SPK[0].n), (SPK[1], SPK[1].n)],
                    inc=KT4[k4], k=16)
                T_KT4.append((KT4[k4], t_kt[1]))
            T_SPQ_ALL = [(SPQ[0], SPQ[0].n), (SPQ[1], SPQ[1].n)]
            wf_view = wffnT_d.rearrange("(c p) f -> c p f", p=P)

            qt_free = [NONE_TOK, NONE_TOK]
            att_free = [NONE_TOK, NONE_TOK]
            atps_free = NONE_TOK
            exp_free = [NONE_TOK, NONE_TOK]
            sc_free = [NONE_TOK, NONE_TOK]
            rs_free = NONE_TOK
            rb_free = NONE_TOK
            rinv_free = NONE_TOK
            rinvb_free = NONE_TOK
            wck_free = [NONE_TOK, NONE_TOK]
            t_wmax = NONE_TOK

            for T in range(NT):
                bi = T % 2
                t_q = dma("sync", SQT[bi], qt_s[bi][:], qT_sp[T],
                          waits=[qt_free[bi]] + T_SPQ_ALL)
                TQ = (SQT[bi], t_q[1])
                t_wc = dma("scalar", SWC[bi], wchunk[bi][:], wf_view[T],
                           waits=[wck_free[bi]] + (guard if T < 2 else []))
                t_cm = Sh.op("vector", lambda e, b=bi: e.tensor_reduce(
                    cmax[:], wchunk[b][:], axis=AX.X, op=ALU.max,
                    apply_absolute_value=True),
                    [(SWC[bi], t_wc[1]), t_wmax], inc=VE)
                if T == 0:
                    t_mxc = Sh.op("vector", lambda e: e.tensor_copy(
                        out=wmax[:], in_=cmax[:]), [(VE, t_cm[1])], inc=VE)
                else:
                    t_mxc = Sh.op("vector", lambda e: e.tensor_tensor(
                        out=wmax[:], in0=wmax[:], in1=cmax[:], op=ALU.max),
                        [(VE, t_cm[1])], inc=VE)
                t_wmax = (VE, t_mxc[1])
                wck_free[bi] = (VE, t_cm[1])

                last_av_tile = None
                last_ev = None
                for half in range(2):
                    last_av = None
                    last_rs = None
                    pend = None

                    def emit_av_rs(jb, ebi, t_ready, T=T, half=half):
                        nonlocal last_av, last_rs
                        for g in range(2):
                            for hl in range(4):
                                h = half * 8 + g * 4 + hl
                                loc = g * 4 + hl
                                w = []
                                if g == 0 and hl == 0:
                                    w = [t_ready]
                                    if jb == 0:
                                        w.append(atps_free)
                                last_av = Sh.op(
                                    "tensor",
                                    lambda e, h=h, loc=loc, jb=jb, ebi=ebi, T=T:
                                    e.matmul(at_ps[:, loc * 128:(loc + 1) * 128],
                                             v_all[:, jb * E + h * 128:
                                                   jb * E + (h + 1) * 128],
                                             expT[ebi][:, loc * 128:(loc + 1) * 128],
                                             start=(jb == 0 and loc % 4 == 0),
                                             stop=(jb == T)),
                                    w, inc=PEm if (g == 1 and hl == 3) else None)
                        for g in range(2):
                            w = [(VE, t_ones[1])]
                            if jb == 0 and g == 0:
                                w.append(rs_free)
                            last_rs = Sh.op(
                                "tensor", lambda e, g=g, ebi=ebi, jb=jb, T=T:
                                e.matmul(rs_ps[:, g * 512:(g + 1) * 512],
                                         onescol[:],
                                         expT[ebi][:, g * 512:(g + 1) * 512],
                                         start=(jb == 0), stop=(jb == T)),
                                w, inc=PEm if g == 1 else None)
                        exp_free[ebi] = (PEm, last_rs[1])

                    for jb in range(T + 1):
                        ebi = jb % 2
                        exp_last = None
                        for g in range(2):
                            mm_last = None
                            for hl in range(4):
                                h = half * 8 + g * 4 + hl
                                w = []
                                if hl == 0:
                                    w = [sc_free[g], TQ, T_KT4[jb // 4]]
                                mm_last = Sh.op(
                                    "tensor", lambda e, h=h, hl=hl, jb=jb, g=g, b=bi:
                                    e.matmul(sc_ps[g][:, hl * 128:(hl + 1) * 128],
                                             kT_all[:, jb * E + h * 128:
                                                    jb * E + (h + 1) * 128],
                                             qt_s[b][:, h * 128:(h + 1) * 128],
                                             start=True, stop=True),
                                    w, inc=PEm if hl == 3 else None)
                            exp_last = Sh.op(
                                "scalar", lambda e, g=g, ebi=ebi:
                                e.activation(out=expT[ebi][:, g * 512:(g + 1) * 512],
                                             in_=sc_ps[g][:, 0:512], func=AF.Exp),
                                [(PEm, mm_last[1]),
                                 exp_free[ebi] if g == 0 else NONE_TOK], inc=ACT)
                            sc_free[g] = (ACT, exp_last[1])
                        t_ready = (ACT, exp_last[1])
                        if jb == T:
                            mb = maskTb[:].unsqueeze(1).broadcast_to([P, 8, P])
                            ev = expT[ebi][:].rearrange("p (g c) -> p g c", c=P)
                            t_m = Sh.op("vector", lambda e, ev=ev, mb=mb:
                                        e.tensor_tensor(out=ev, in0=ev, in1=mb,
                                                        op=ALU.mult),
                                        [t_ready, T_CONST], inc=VE)
                            t_ready = (VE, t_m[1])
                        if pend is not None:
                            emit_av_rs(*pend)
                        pend = (jb, ebi, t_ready)
                    emit_av_rs(*pend)

                    t_ri = Sh.op("vector", lambda e: e.reciprocal(rinv[:], rs_ps[:]),
                                 [(PEm, last_rs[1]), rinv_free], inc=VE)
                    rs_free = (VE, t_ri[1])
                    t_rb_last = None
                    for g in range(2):
                        w = [(VE, t_ri[1]), (VE, t_onesr[1])]
                        if g == 0:
                            w.append(rb_free)
                        t_rb_last = Sh.op("tensor", lambda e, g=g:
                                          e.matmul(rb_ps[:, g * 512:(g + 1) * 512],
                                                   onesrow[:],
                                                   rinv[:, g * 512:(g + 1) * 512],
                                                   start=True, stop=True),
                                          w, inc=PEm)
                    t_rbe = Sh.op("vector", lambda e: e.tensor_copy(
                        out=rinvb[:], in_=rb_ps[:]),
                        [(PEm, t_rb_last[1]), rinvb_free], inc=VE)
                    rb_free = (VE, t_rbe[1])
                    rinv_free = (VE, t_rbe[1])
                    ev_last = None
                    for g in range(2):
                        ev_last = Sh.op("vector", lambda e, g=g, half=half, b=bi:
                                        e.tensor_tensor(
                                            out=attnT_t[b][:, half * 1024 + g * 512:
                                                           half * 1024 + (g + 1) * 512],
                                            in0=at_ps[:, g * 512:(g + 1) * 512],
                                            in1=rinvb[:, g * 512:(g + 1) * 512],
                                            op=ALU.mult),
                                        [(PEm, last_av[1]), (VE, t_rbe[1]),
                                         att_free[bi] if g == 0 else NONE_TOK],
                                        inc=VE)
                    rinvb_free = (VE, ev_last[1])
                    atps_free = (VE, ev_last[1])
                    last_av_tile = last_av
                    last_ev = ev_last
                qt_free[bi] = (PEm, last_av_tile[1])
                tsp = dma("scalar", SPA[bi], attnT_sp[T], attnT_t[bi][:],
                          waits=[(VE, last_ev[1])])
                t_attnT_sp[T] = tsp
                att_free[bi] = (SPA[bi], tsp[1])

        vsc.close()

        with ExitStack() as p4:
            e4 = p4.enter_context
            wo_sb = e4(nc.sbuf_tensor("wo_sb", [P, NCH * E], BF16))
            sg_sb = e4(nc.sbuf_tensor("sg_sb", [P, NCH * E], BF16))
            bout_b = e4(nc.sbuf_tensor("bout_b", [P, E], F32))
            gamS_b = e4(nc.sbuf_tensor("gamS_b", [P, E], F32))
            beta_b = e4(nc.sbuf_tensor("beta_b", [P, E], F32))

            guard = phase_guard()
            wo_view = woutT_d.rearrange("(c p) f -> c p f", p=P)
            for c in range(NCH):
                dma("gpsimd", WO4[c // 4], wo_sb[:, c * E:(c + 1) * E], wo_view[c],
                    waits=guard if c == 0 else ())
            T_WO = [(WO4[k], WO4[k].n) for k in range(4)]
            for c in range(NCH):
                dma("gpsimd", SG4[c // 4], sg_sb[:, c * E:(c + 1) * E], wf_view[c],
                    waits=guard if c == 0 else ())
            T_SG = [(SG4[k], SG4[k].n) for k in range(4)]
            t_sign = None
            sign_q = [None] * 4
            for c in range(NCH):
                t_sign = Sh.op("scalar", lambda e, c=c: e.activation(
                    out=sg_sb[:, c * E:(c + 1) * E],
                    in_=sg_sb[:, c * E:(c + 1) * E], func=AF.Sign),
                    [T_SG[c // 4]], inc=ACT)
                if c % 4 == 3:
                    sign_q[c // 4] = (ACT, t_sign[1])
            sign_done = (ACT, t_sign[1])

            with ExitStack() as p4a:
                e4a = p4a.enter_context
                mpad = e4a(nc.sbuf_tensor("mpad", [P, P], F32))
                mrow = e4a(nc.sbuf_tensor("mrow", [1, P], F32))
                msc = e4a(nc.sbuf_tensor("msc", [1, 1], F32))
                gs_cols = e4a(nc.sbuf_tensor("gs_cols", [P, NCH], F32))
                msc_b = e4a(nc.sbuf_tensor("msc_b", [P, 1], F32))
                mt_ps = e4a(nc.psum_tensor("mt_ps", [P, P], F32))
                bc_ps4 = e4a(nc.psum_tensor("bc_ps4", [P, 512], F32))

                t_mz = Sh.op("vector", lambda e: e.memset(mpad[:], 0.0),
                             guard, inc=VE)
                t_mp = Sh.op("vector", lambda e: e.tensor_copy(
                    out=mpad[:, 0:1], in_=wmax[:]),
                    [t_wmax, (VE, t_mz[1])], inc=VE)
                t_tr = Sh.op("tensor", lambda e: e.transpose(mt_ps[:], mpad[:],
                                                             identf[:]),
                             [(VE, t_mp[1]), (VE, t_idf[1])] + guard, inc=PEm)
                t_mr = Sh.op("vector", lambda e: e.tensor_copy(
                    out=mrow[:], in_=mt_ps[0:1, :]), [(PEm, t_tr[1])], inc=VE)
                t_ms = Sh.op("vector", lambda e: e.tensor_reduce(
                    msc[:], mrow[:], axis=AX.X, op=ALU.max),
                    [(VE, t_mr[1])], inc=VE)
                t_msb_pe = Sh.op("tensor", lambda e: e.matmul(
                    mt_ps[:, 0:1], onesrow[:], msc[:], start=True, stop=True),
                    [(VE, t_ms[1]), (VE, t_onesr[1])], inc=PEm)
                t_msb = Sh.op("vector", lambda e: e.tensor_copy(
                    out=msc_b[:], in_=mt_ps[:, 0:1]), [(PEm, t_msb_pe[1])], inc=VE)
                t_gs = Sh.op("vector", lambda e: e.tensor_scalar(
                    out=gs_cols[:], in0=gamma_c[:], scalar1=msc_b[:], scalar2=None,
                    op0=ALU.mult), [(VE, t_msb[1]), T_CONST], inc=VE)
                t_gsb = build_bcast(gamS_b, gs_cols[:], 0, bc_ps4,
                                    extra_waits=[(VE, t_gs[1])] + guard)
                t_bb4 = build_bcast(beta_b, beta_c[:], 0, bc_ps4)
                t_bob = build_bcast(bout_b, bout_c[:], 0, bc_ps4)
            p4a_guard = phase_guard()

            at_s = [e4(nc.sbuf_tensor(f"at_s{i}", [P, E], BF16)) for i in range(2)]
            x2t_s = [e4(nc.sbuf_tensor(f"x2t_s{i}", [P, E], BF16)) for i in range(2)]
            xb4_t = e4(nc.sbuf_tensor("xb4_0", [P, E], F32))
            xb4 = [xb4_t, xb4_t]
            s_f_t = e4(nc.sbuf_tensor("s_f", [P, E], F32))
            s_f = [s_f_t, s_f_t]
            f_f_t = e4(nc.sbuf_tensor("f_f", [P, E], F32))
            f_f = [f_f_t, f_f_t]
            g_f = e4(nc.sbuf_tensor("g_f", [P, E], F32))
            s_ps = e4(nc.psum_tensor("s_ps", [P, E], F32))
            f_ps = e4(nc.psum_tensor("f_ps", [P, E], F32))

            T_SPA_ALL = [(SPA[0], SPA[0].n), (SPA[1], SPA[1].n)]
            T_SP2_ALL = [(SP2[0], SP2[0].n), (SP2[1], SP2[1].n)]
            at_free = [NONE_TOK, NONE_TOK]
            x2t_free = [NONE_TOK, NONE_TOK]
            xb4_free = NONE_TOK
            sps_free = list(p4a_guard[:3])
            fps_free = list(p4a_guard[:3])
            sf_free = NONE_TOK
            sfx_free = NONE_TOK
            ff_free = NONE_TOK
            ffx_free = NONE_TOK
            gf_free = NONE_TOK
            t_out_last = None
            for T in range(NT):
                bi = T % 2
                afb = at_free[bi] if isinstance(at_free[bi], list) else [at_free[bi]]
                t_at = dma("sync", SAT[bi], at_s[bi][:], attnT_sp[T],
                           waits=afb + T_SPA_ALL + p4a_guard)
                t_x2 = dma("sync", SX2[bi], x2t_s[bi][:], xn2T_sp[T],
                           waits=[x2t_free[bi]] + T_SP2_ALL)
                t_xb = dma("sync", SXB, xb4[bi][:], x_d[T * P:(T + 1) * P, :],
                           waits=[xb4_free])
                mm = None
                for ec in range(NCH):
                    w = [(SAT[bi], t_at[1])]
                    if ec == 0:
                        w += list(sps_free) + [T_WO[0]]
                    elif ec % 4 == 0:
                        w += [T_WO[ec // 4]]
                    for fc in range(4):
                        mm = Sh.op("tensor", lambda e, ec=ec, fc=fc, b=bi:
                                   e.matmul(s_ps[:, fc * 512:(fc + 1) * 512],
                                            at_s[b][:, ec * 128:(ec + 1) * 128],
                                            wo_sb[:, ec * E + fc * 512:
                                                  ec * E + (fc + 1) * 512],
                                            start=(ec == 0), stop=(ec == NCH - 1)),
                                   w if fc == 0 else [],
                                   inc=PEm if ec == NCH - 1 else None)
                t_smm = (PEm, mm[1])
                if debug:
                    td = dma("gpsimd", SDS, dbg["at"][T], at_s[bi][:],
                             waits=[(SAT[bi], t_at[1])])
                    at_free[bi] = [t_smm, (SDS, td[1])]
                else:
                    at_free[bi] = t_smm
                evs_last = None
                for fc in range(4):
                    evs_last = Sh.op("vector", lambda e, fc=fc, b=bi: e.tensor_tensor(
                        out=s_f[b][:, fc * 512:(fc + 1) * 512],
                        in0=s_ps[:, fc * 512:(fc + 1) * 512],
                        in1=bout_b[:, fc * 512:(fc + 1) * 512], op=ALU.add),
                        [t_smm, (VE, t_bob[1])] +
                        ([sf_free, sfx_free] if fc == 0 else []), inc=VE)
                t_sf = (VE, evs_last[1])
                if debug:
                    td = dma("gpsimd", SDS, dbg["s"][T * P:(T + 1) * P, :],
                             s_f[bi][:], waits=[t_sf])
                    sf_x = (SDS, td[1])
                else:
                    sf_x = NONE_TOK
                mm = None
                for ec in range(NCH):
                    w = [(SX2[bi], t_x2[1])]
                    if ec == 0:
                        w += list(fps_free) + [sign_q[0]]
                    elif ec % 4 == 0:
                        w += [sign_q[ec // 4]]
                    for fc in range(4):
                        mm = Sh.op("tensor", lambda e, ec=ec, fc=fc, b=bi:
                                   e.matmul(f_ps[:, fc * 512:(fc + 1) * 512],
                                            x2t_s[b][:, ec * 128:(ec + 1) * 128],
                                            sg_sb[:, ec * E + fc * 512:
                                                  ec * E + (fc + 1) * 512],
                                            start=(ec == 0), stop=(ec == NCH - 1)),
                                   w if fc == 0 else [],
                                   inc=PEm if ec == NCH - 1 else None)
                t_fmm = (PEm, mm[1])
                x2t_free[bi] = t_fmm
                evs_last = None
                for fc in range(4):
                    evs_last = Sh.op("vector", lambda e, fc=fc, b=bi: e.tensor_tensor(
                        out=f_f[b][:, fc * 512:(fc + 1) * 512],
                        in0=f_ps[:, fc * 512:(fc + 1) * 512],
                        in1=gamS_b[:, fc * 512:(fc + 1) * 512], op=ALU.mult),
                        [t_fmm, (VE, t_gsb[1])] +
                        ([ff_free, ffx_free] if fc == 0 else []), inc=VE)
                fps_free = [(VE, evs_last[1])]
                t_fb = Sh.op("vector", lambda e, b=bi: e.tensor_tensor(
                    out=f_f[b][:], in0=f_f[b][:], in1=beta_b[:], op=ALU.add),
                    [(VE, t_bb4[1])], inc=VE)
                if debug:
                    td = dma("gpsimd", SDF, dbg["f"][T * P:(T + 1) * P, :],
                             f_f[bi][:], waits=[(VE, t_fb[1])])
                    ff_x = (SDF, td[1])
                else:
                    ff_x = NONE_TOK
                t_g2 = Sh.op("scalar", lambda e, b=bi: e.activation(
                    out=g_f[:], in_=f_f[b][:], func=AF.Silu),
                    [(VE, t_fb[1]), gf_free], inc=ACT)
                ff_free = (ACT, t_g2[1])
                ffx_free = ff_x
                if debug:
                    td = dma("gpsimd", SDG, dbg["g"][T * P:(T + 1) * P, :],
                             g_f[:], waits=[(ACT, t_g2[1])])
                    gf_x = (SDG, td[1])
                else:
                    gf_x = NONE_TOK
                t_sg = Sh.op("vector", lambda e, b=bi: e.tensor_tensor(
                    out=g_f[:], in0=g_f[:], in1=s_f[b][:], op=ALU.mult),
                    [(ACT, t_g2[1]), t_sf, gf_x], inc=VE)
                t_o = Sh.op("vector", lambda e, b=bi: e.tensor_tensor(
                    out=s_f[b][:], in0=g_f[:], in1=xb4[b][:], op=ALU.add),
                    [(VE, t_sg[1]), (SXB, t_xb[1]), sf_x], inc=VE)
                gf_free = (VE, t_o[1])
                xb4_free = (VE, t_o[1])
                sps_free = [t_sf]
                t_st = dma("gpsimd", OUTS, out_d[T * P:(T + 1) * P, :], s_f[bi][:],
                           waits=[(VE, t_o[1])])
                sf_free = (OUTS, t_st[1])
                sfx_free = sf_x
                t_out_last = t_st

            if debug:
                fin = [(OUTS, 16 * NT), (VE, VE.n), (ACT, ACT.n), (PEm, PEm.n)]
                dma("gpsimd", SDS, dbg["sg_sb"], sg_sb[:], waits=fin)
                dma("gpsimd", SDS, dbg["wo_sb"], wo_sb[:], waits=fin)
                dma("gpsimd", SDS, dbg["bout_b"], bout_b[:], waits=fin)
                dma("gpsimd", SDS, dbg["gamS_b"], gamS_b[:], waits=fin)
                dma("gpsimd", SDS, dbg["beta_b"], beta_b[:], waits=fin)

            Sh.wait("gpsimd", (OUTS, 16 * NT))
            Sh.wait("sync", (OUTS, 16 * NT))

        blk = ctx.enter_context(nc.Block())
        Sh.replay(blk)

    return nc


_CONSTS = None


def _consts():
    global _CONSTS
    if _CONSTS is None:
        inv = 10000.0 ** (-(np.arange(0, E, 2, dtype=np.float64) / E))
        emb = np.repeat(inv, 2)
        sin = np.sin(S * emb).astype(np.float32).reshape(1, E)
        cos = np.cos(S * emb).astype(np.float32).reshape(1, E)
        ident = np.eye(P).astype(ml_dtypes.bfloat16)
        jj = np.arange(P).reshape(P, 1)
        ii = np.arange(P).reshape(1, P)
        maskT = (jj <= ii).astype(ml_dtypes.bfloat16)
        _CONSTS = (sin, cos, ident, maskT)
    return _CONSTS


def make_in_maps(L0_attn_x, L0_attn_Wqkv, L0_attn_bqkv, L0_attn_Wout, L0_attn_bout,
                 L0_ffn_weight, L0_ffn_gamma, L0_ffn_beta, L0_ffn_ln_weight,
                 L0_ffn_ln_bias):
    x = np.ascontiguousarray(np.asarray(L0_attn_x, dtype=np.float32))
    wqkvT = np.ascontiguousarray(np.asarray(L0_attn_Wqkv, np.float32).T)
    woutT = np.ascontiguousarray(np.asarray(L0_attn_Wout, np.float32).T)
    wffnT = np.ascontiguousarray(np.asarray(L0_ffn_weight, np.float32).T)

    def cols(a, n):
        return np.ascontiguousarray(np.asarray(a, np.float32).reshape(n, P).T)

    sin, cos, ident, maskT = _consts()
    shared = dict(wqkvT=wqkvT, woutT=woutT, wffnT=wffnT,
                  bqkv_c=cols(L0_attn_bqkv, 3 * NCH),
                  bout_c=cols(L0_attn_bout, NCH),
                  gamma_c=cols(L0_ffn_gamma, NCH),
                  beta_c=cols(L0_ffn_beta, NCH),
                  lnw_c=cols(L0_ffn_ln_weight, NCH),
                  lnb_c=cols(L0_ffn_ln_bias, NCH),
                  cos_c=cols(cos, NCH), sin_c=cols(sin, NCH),
                  identb=ident, maskTb=maskT)
    return [dict(x=np.ascontiguousarray(x[b]), **shared) for b in range(x.shape[0])]


def kernel(L0_attn_x, L0_attn_Wqkv, L0_attn_bqkv, L0_attn_Wout, L0_attn_bout,
           L0_ffn_weight, L0_ffn_gamma, L0_ffn_beta, L0_ffn_ln_weight,
           L0_ffn_ln_bias):
    in_maps = make_in_maps(L0_attn_x, L0_attn_Wqkv, L0_attn_bqkv, L0_attn_Wout,
                           L0_attn_bout, L0_ffn_weight, L0_ffn_gamma, L0_ffn_beta,
                           L0_ffn_ln_weight, L0_ffn_ln_bias)
    nc = build_nc()
    res = run_bass_kernel_spmd(nc, in_maps, core_ids=list(range(8)))
    return np.stack([np.asarray(res.results[b]["out"]) for b in range(8)], axis=0)


# version 05
# speedup vs baseline: 1.0119x; 1.0119x over previous
import numpy as np
import ml_dtypes
from contextlib import ExitStack

import concourse.bass as bass
import concourse.mybir as mybir
from concourse.bass_utils import run_bass_kernel_spmd

F32 = mybir.dt.float32
BF16 = mybir.dt.bfloat16
AF = mybir.ActivationFunctionType
ALU = mybir.AluOpType
AX = mybir.AxisListType

P = 128
S = 2048
E = 2048
NT = S // P
NCH = E // P
EPS = 1e-5
QKSCALE = float(E) ** (-0.25)
LNQK = float(np.log(QKSCALE))

ENGINES = ("sync", "scalar", "vector", "tensor", "gpsimd")
NONE_TOK = (None, 0)


class Sem:
    def __init__(self, ctx, nc, name, registry):
        self.h = ctx.enter_context(nc.semaphore(name))
        self.n = 0
        self.name = name
        registry.append(self)


class Sched:
    def __init__(self):
        self.q = {e: [] for e in ENGINES}
        self.last_wait = {e: {} for e in ENGINES}

    def op(self, eng, build, waits=(), inc=None, k=1):
        wl = []
        lw = self.last_wait[eng]
        for (s, v) in waits:
            if s is None or v <= 0:
                continue
            if lw.get(s.name, -1) >= v:
                continue
            lw[s.name] = v
            wl.append((s, v))
        post = None
        if inc is not None:
            inc.n += k
            post = inc.n

        def emit(engine, wl=wl, build=build, inc=inc, k=k):
            for (s, v) in wl:
                engine.wait_ge(s.h, v)
            ins = build(engine)
            if inc is not None:
                ins.then_inc(inc.h, k)

        self.q[eng].append(emit)
        return (inc, post)

    def wait(self, eng, *tokens):
        wl = []
        lw = self.last_wait[eng]
        for (s, v) in tokens:
            if s is None or v <= 0 or lw.get(s.name, -1) >= v:
                continue
            lw[s.name] = v
            wl.append((s, v))
        if not wl:
            return

        def emit(engine, wl=wl):
            for (s, v) in wl:
                engine.wait_ge(s.h, v)

        self.q[eng].append(emit)

    def replay(self, block):
        for eng in ENGINES:
            thunks = self.q[eng]
            if not thunks:
                continue

            def run(engine, thunks=thunks):
                for t in thunks:
                    t(engine)

            getattr(block, eng)(run)


def build_nc(debug=False):
    nc = bass.Bass("TRN2", target_bir_lowering=False, debug=False,
                   dynamic_dma_scratch_size=16384)

    x_d = nc.dram_tensor("x", [S, E], F32, kind="ExternalInput").ap()
    wqkvT_d = nc.dram_tensor("wqkvT", [E, 3 * E], F32, kind="ExternalInput").ap()
    woutT_d = nc.dram_tensor("woutT", [E, E], F32, kind="ExternalInput").ap()
    wffnT_d = nc.dram_tensor("wffnT", [E, E], F32, kind="ExternalInput").ap()
    bqkv_d = nc.dram_tensor("bqkv_c", [P, 3 * NCH], F32, kind="ExternalInput").ap()
    bout_d = nc.dram_tensor("bout_c", [P, NCH], F32, kind="ExternalInput").ap()
    gamma_d = nc.dram_tensor("gamma_c", [P, NCH], F32, kind="ExternalInput").ap()
    beta_d = nc.dram_tensor("beta_c", [P, NCH], F32, kind="ExternalInput").ap()
    lnw_c_d = nc.dram_tensor("lnw_c", [P, NCH], F32, kind="ExternalInput").ap()
    lnb_c_d = nc.dram_tensor("lnb_c", [P, NCH], F32, kind="ExternalInput").ap()
    cos_d = nc.dram_tensor("cos_c", [P, NCH], F32, kind="ExternalInput").ap()
    sin_d = nc.dram_tensor("sin_c", [P, NCH], F32, kind="ExternalInput").ap()
    ident_d = nc.dram_tensor("identb", [P, P], BF16, kind="ExternalInput").ap()
    maskT_d = nc.dram_tensor("maskTb", [P, P], BF16, kind="ExternalInput").ap()

    out_d = nc.dram_tensor("out", [S, E], F32, kind="ExternalOutput").ap()

    skind = "ExternalOutput" if debug else "Internal"
    xnT_sp = nc.dram_tensor("xnT_sp", [NT, P, E], BF16, kind=skind).ap()
    xn2T_sp = nc.dram_tensor("xn2T_sp", [NT, P, E], BF16, kind=skind).ap()
    qT_sp = nc.dram_tensor("qT_sp", [NT, P, E], BF16, kind=skind).ap()
    kT_sp = nc.dram_tensor("kT_sp", [NT, P, E], BF16, kind=skind).ap()
    attnT_sp = nc.dram_tensor("attnT_sp", [NT, P, E], BF16, kind=skind).ap()
    dbg = {}
    if debug:
        dbg["sg_sb"] = nc.dram_tensor("d_sg", [P, NCH * E], BF16,
                                      kind="ExternalOutput").ap()
        dbg["s"] = nc.dram_tensor("d_s", [S, E], F32, kind="ExternalOutput").ap()
        dbg["at"] = nc.dram_tensor("d_at", [NT, P, E], BF16, kind="ExternalOutput").ap()
        dbg["wo_sb"] = nc.dram_tensor("d_wo", [P, NCH * E], BF16, kind="ExternalOutput").ap()
        dbg["bout_b"] = nc.dram_tensor("d_bout", [P, E], F32, kind="ExternalOutput").ap()
        dbg["gamS_b"] = nc.dram_tensor("d_gam", [P, E], F32, kind="ExternalOutput").ap()
        dbg["beta_b"] = nc.dram_tensor("d_beta", [P, E], F32, kind="ExternalOutput").ap()
        dbg["f"] = nc.dram_tensor("d_f", [S, E], F32, kind="ExternalOutput").ap()
        dbg["g"] = nc.dram_tensor("d_g", [S, E], F32, kind="ExternalOutput").ap()

    Sh = Sched()
    wq_view = wqkvT_d.rearrange("(c p) f -> c p f", p=P)

    with ExitStack() as ctx:
        SEMS = []

        def sem(name):
            return Sem(ctx, nc, name, SEMS)

        def sem2(name):
            return [sem(name + "0"), sem(name + "1")]

        PEm = sem("s_pe")
        VE = sem("s_ve")
        ACT = sem("s_ac")
        CONSTS = sem("s_consts")
        WGRP = [sem(f"s_wg{i}") for i in range(6)]
        WGRPB = [sem(f"s_wgb{i}") for i in range(6)]
        WO4 = [sem(f"s_wo4_{i}") for i in range(4)]
        SG4 = [sem(f"s_sg4_{i}") for i in range(4)]

        def wsem(g, c):
            return WGRP[g] if c < NCH // 2 else WGRPB[g]
        SX = sem2("s_x")
        SPX = sem2("s_spx")
        SP2 = sem2("s_sp2")
        SPQ = sem2("s_spq")
        SPK = sem2("s_spk")
        SPA = sem2("s_spa")
        SQX = sem2("s_qx")
        SQT = sem2("s_qt")
        SAT = sem2("s_at")
        SX2 = sem2("s_x2")
        SXB = sem("s_xb")
        SWC = sem2("s_wc")
        OUTS = sem("s_outs")
        SDS = sem("s_ds")
        SDF = sem("s_df")
        SDG = sem("s_dg")

        def dma(eng, semm, out_ap, in_ap, waits=()):
            return Sh.op(eng, lambda e, o=out_ap, i=in_ap: e.dma_start(out=o, in_=i),
                         waits, inc=semm, k=16)

        def phase_guard():
            return [(s, s.n) for s in SEMS]

        identb = ctx.enter_context(nc.sbuf_tensor("sb_identb", [P, P], BF16))
        maskTb = ctx.enter_context(nc.sbuf_tensor("sb_maskTb", [P, P], BF16))
        onescol = ctx.enter_context(nc.sbuf_tensor("onescol", [P, 1], BF16))
        onesrow = ctx.enter_context(nc.sbuf_tensor("onesrow", [1, P], F32))
        epsb = ctx.enter_context(nc.sbuf_tensor("epsb", [P, 1], F32))
        eps24b = ctx.enter_context(nc.sbuf_tensor("eps24b", [P, 1], F32))
        lqkb = ctx.enter_context(nc.sbuf_tensor("lqkb", [P, 1], F32))
        wmax = ctx.enter_context(nc.sbuf_tensor("sb_wmax", [P, 1], F32))
        lnw_c = ctx.enter_context(nc.sbuf_tensor("sb_lnw_c", [P, NCH], F32))
        lnb_c = ctx.enter_context(nc.sbuf_tensor("sb_lnb_c", [P, NCH], F32))
        bqkv_c = ctx.enter_context(nc.sbuf_tensor("sb_bqkv_c", [P, 3 * NCH], F32))
        bout_c = ctx.enter_context(nc.sbuf_tensor("sb_bout_c", [P, NCH], F32))
        gamma_c = ctx.enter_context(nc.sbuf_tensor("sb_gamma_c", [P, NCH], F32))
        beta_c = ctx.enter_context(nc.sbuf_tensor("sb_beta_c", [P, NCH], F32))
        cos_c = ctx.enter_context(nc.sbuf_tensor("sb_cos_c", [P, NCH], F32))
        sin_c = ctx.enter_context(nc.sbuf_tensor("sb_sin_c", [P, NCH], F32))
        identf = ctx.enter_context(nc.sbuf_tensor("sb_identf", [P, P], F32))

        for dst, src in [(identb, ident_d), (maskTb, maskT_d), (lnw_c, lnw_c_d),
                         (lnb_c, lnb_c_d), (bqkv_c, bqkv_d), (bout_c, bout_d),
                         (gamma_c, gamma_d), (beta_c, beta_d), (cos_c, cos_d),
                         (sin_c, sin_d)]:
            dma("sync", CONSTS, dst[:], src)
        T_CONST = (CONSTS, CONSTS.n)

        t_ones = Sh.op("vector", lambda e: e.memset(onescol[:], 1.0), (), inc=VE)
        t_onesr = Sh.op("vector", lambda e: e.memset(onesrow[:], 1.0), (), inc=VE)
        t_eps = Sh.op("vector", lambda e: e.memset(epsb[:], EPS), (), inc=VE)
        Sh.op("vector", lambda e: e.memset(eps24b[:], 1e-24), (), inc=None)
        Sh.op("vector", lambda e: e.memset(lqkb[:], LNQK), (), inc=None)
        t_idf = Sh.op("vector", lambda e: e.tensor_copy(out=identf[:], in_=identb[:]),
                      [T_CONST], inc=VE)

        bc_state = {"free": NONE_TOK}

        def build_bcast(dst, cols_ap, c0, bc_ps, nch=NCH, extra_waits=()):
            last = None
            for c4 in range(nch // 4):
                tpe = None
                for j in range(4):
                    c = c4 * 4 + j
                    lhs = cols_ap[:, c0 + c:c0 + c + 1].broadcast_to([P, P])
                    w = [] if j else (list(extra_waits) +
                                      [bc_state["free"], (VE, t_idf[1]), T_CONST])
                    tpe = Sh.op("tensor",
                                lambda e, lhs=lhs, j=j, bc_ps=bc_ps:
                                e.matmul(bc_ps[:, j * 128:(j + 1) * 128],
                                         lhs, identf[:], start=True, stop=True),
                                w, inc=PEm if j == 3 else None)
                tve = Sh.op("vector",
                            lambda e, c4=c4, dst=dst, bc_ps=bc_ps:
                            e.tensor_copy(out=dst[:, c4 * 512:(c4 + 1) * 512],
                                          in_=bc_ps[:, 0:512]),
                            [(PEm, tpe[1])], inc=VE)
                bc_state["free"] = (VE, tve[1])
                last = tve
            return last

        def transpose16(tr_ps, tr_state, src, src_toks, evac_fn):
            last_ev = None
            last_pe = None
            for half in range(2):
                tps_last = None
                for j in range(8):
                    c = half * 8 + j
                    w = []
                    if j == 0:
                        w = list(src_toks) + [tr_state["free"][half], T_CONST]
                    tps_last = Sh.op("tensor", lambda e, c=c, j=j, half=half,
                                     src=src, tr_ps=tr_ps:
                                     e.transpose(
                                         tr_ps[:, half * 1024 + j * 128:
                                               half * 1024 + (j + 1) * 128],
                                         src[:, c * 128:(c + 1) * 128],
                                         identb[:]),
                                     w, inc=PEm if j == 7 else None)
                tev = evac_fn(half, (PEm, tps_last[1]))
                tr_state["free"][half] = tev
                last_ev = tev
                last_pe = tps_last
            return last_ev, (PEm, last_pe[1])

        t_xnT_sp = [None] * NT
        t_xn2T_sp = [None] * NT
        t_attnT_sp = [None] * NT
        qk_spill = {"q": [None] * NT, "k": [None] * NT}

        vsc = ExitStack()
        v_all = vsc.enter_context(nc.sbuf_tensor("v_all", [P, NT * E], BF16))

        with ExitStack() as p1:
            e1 = p1.enter_context
            xbuf = [e1(nc.sbuf_tensor(f"xbuf{i}", [P, E], F32)) for i in range(2)]
            bn24 = [e1(nc.sbuf_tensor(f"bn24_{i}", [P, 48], F32)) for i in range(2)]
            mv1 = [e1(nc.sbuf_tensor(f"mv1_{i}", [P, 2], F32)) for i in range(2)]
            mv2 = [e1(nc.sbuf_tensor(f"mv2_{i}", [P, 2], F32)) for i in range(2)]
            rstd1 = [e1(nc.sbuf_tensor(f"rstd1_{i}", [P, 1], F32)) for i in range(2)]
            rstd2 = [e1(nc.sbuf_tensor(f"rstd2_{i}", [P, 1], F32)) for i in range(2)]
            negm1 = [e1(nc.sbuf_tensor(f"negm1_{i}", [P, 1], F32)) for i in range(2)]
            negm2 = [e1(nc.sbuf_tensor(f"negm2_{i}", [P, 1], F32)) for i in range(2)]
            xn = [e1(nc.sbuf_tensor(f"xn{i}", [P, E], F32)) for i in range(2)]
            xnb = [e1(nc.sbuf_tensor(f"xnb{i}", [P, E], BF16)) for i in range(2)]
            xn2b = [e1(nc.sbuf_tensor(f"xn2b{i}", [P, E], BF16)) for i in range(2)]
            xnT_t = [e1(nc.sbuf_tensor(f"xnT_t{i}", [P, E], BF16)) for i in range(2)]
            xn2T_t = [e1(nc.sbuf_tensor(f"xn2T_t{i}", [P, E], BF16)) for i in range(2)]
            wv_sb = e1(nc.sbuf_tensor("wv_sb", [P, NCH * E], BF16))
            bv_b = e1(nc.sbuf_tensor("bv_b", [P, E], F32))
            tr_ps = e1(nc.psum_tensor("tr_ps", [P, 2048], BF16))
            v_ps = e1(nc.psum_tensor("v_ps", [P, E], F32))
            bc_ps = e1(nc.psum_tensor("bc_ps", [P, 512], F32))

            for c in range(NCH):
                dma("gpsimd", wsem(0, c), wv_sb[:, c * E:(c + 1) * E],
                    wq_view[c, :, 4096:6144])
            T_WV = [(WGRP[0], WGRP[0].n), (WGRPB[0], WGRPB[0].n)]
            t_bvb = build_bcast(bv_b, bqkv_c[:], 2 * NCH, bc_ps)

            tr_state = {"free": [NONE_TOK, NONE_TOK]}
            xbuf_free = [NONE_TOK, NONE_TOK]
            xnT_free = [NONE_TOK, NONE_TOK]
            xn2T_free = [NONE_TOK, NONE_TOK]
            vps_free = NONE_TOK
            mv1_g = [NONE_TOK, NONE_TOK]
            mv2_g = [NONE_TOK, NONE_TOK]
            rstd_g = [NONE_TOK, NONE_TOK]
            prev_pe_xnT = [NONE_TOK, NONE_TOK]
            prev_pe_xn2T = [NONE_TOK, NONE_TOK]

            pend_vevac = []

            def flush_vevac():
                nonlocal vps_free
                while pend_vevac:
                    Tv, t_vmm = pend_vevac.pop(0)
                    evs_last = None
                    for fc in range(4):
                        evs_last = Sh.op("vector", lambda e, fc=fc, Tv=Tv:
                                         e.tensor_tensor(
                                             out=v_all[:, Tv * E + fc * 512:
                                                       Tv * E + (fc + 1) * 512],
                                             in0=v_ps[:, fc * 512:(fc + 1) * 512],
                                             in1=bv_b[:, fc * 512:(fc + 1) * 512],
                                             op=ALU.add),
                                         [t_vmm, (VE, t_bvb[1])], inc=VE)
                    vps_free = (VE, evs_last[1])

            for T in range(NT):
                bi = T % 2
                t_x = dma("sync", SX[bi], xbuf[bi][:], x_d[T * P:(T + 1) * P, :],
                          waits=[xbuf_free[bi]])
                TX = (SX[bi], t_x[1])
                t_bn = None
                for q4 in range(4):
                    t_bn = Sh.op("vector", lambda e, b=bi, q4=q4: e.bn_stats(
                        bn24[b][:, q4 * 6:(q4 + 1) * 6],
                        xbuf[b][:, q4 * 512:(q4 + 1) * 512]),
                        [TX], inc=VE if q4 == 3 else None)
                t_ag = Sh.op("vector", lambda e, b=bi: e.bn_aggr(mv1[b][:],
                                                                 bn24[b][:, 0:24]),
                             [(VE, t_bn[1]), mv1_g[bi]], inc=VE)
                t_nm = Sh.op("vector", lambda e, b=bi: e.tensor_scalar(
                    out=negm1[b][:], in0=mv1[b][:, 0:1], scalar1=-1.0, scalar2=None,
                    op0=ALU.mult), [(VE, t_ag[1])], inc=VE)
                t_l1 = Sh.op("scalar", lambda e, b=bi: e.activation(
                    out=rstd1[b][:], in_=mv1[b][:, 1:2], func=AF.Ln, bias=epsb[:]),
                    [(VE, t_ag[1]), (VE, t_eps[1]), rstd_g[bi]], inc=ACT)
                t_e1 = Sh.op("scalar", lambda e, b=bi: e.activation(
                    out=rstd1[b][:], in_=rstd1[b][:], func=AF.Exp, scale=-0.5),
                    [(ACT, t_l1[1])], inc=ACT)
                mv1_g[bi] = (ACT, t_l1[1])
                t_xn = Sh.op("vector", lambda e, b=bi: e.tensor_scalar(
                    out=xn[b][:], in0=xbuf[b][:], scalar1=negm1[b][:],
                    scalar2=rstd1[b][:], op0=ALU.add, op1=ALU.mult),
                    [(ACT, t_e1[1]), (VE, t_nm[1])], inc=VE)
                xbuf_free[bi] = (VE, t_xn[1])
                t_xnb = Sh.op("vector", lambda e, b=bi: e.tensor_copy(
                    out=xnb[b][:], in_=xn[b][:]),
                    [prev_pe_xnT[bi]], inc=VE)
                t_bn2 = None
                for q4 in range(4):
                    t_bn2 = Sh.op("vector", lambda e, b=bi, q4=q4: e.bn_stats(
                        bn24[b][:, 24 + q4 * 6:24 + (q4 + 1) * 6],
                        xn[b][:, q4 * 512:(q4 + 1) * 512]),
                        [], inc=VE if q4 == 3 else None)
                t_ag2 = Sh.op("vector", lambda e, b=bi: e.bn_aggr(
                    mv2[b][:], bn24[b][:, 24:48]),
                    [(VE, t_bn2[1]), mv2_g[bi]], inc=VE)
                t_nm2 = Sh.op("vector", lambda e, b=bi: e.tensor_scalar(
                    out=negm2[b][:], in0=mv2[b][:, 0:1], scalar1=-1.0, scalar2=None,
                    op0=ALU.mult), [(VE, t_ag2[1])], inc=VE)
                t_l2 = Sh.op("scalar", lambda e, b=bi: e.activation(
                    out=rstd2[b][:], in_=mv2[b][:, 1:2], func=AF.Ln, bias=epsb[:]),
                    [(VE, t_ag2[1])], inc=ACT)
                t_e2 = Sh.op("scalar", lambda e, b=bi: e.activation(
                    out=rstd2[b][:], in_=rstd2[b][:], func=AF.Exp, scale=-0.5),
                    [(ACT, t_l2[1])], inc=ACT)
                mv2_g[bi] = (ACT, t_l2[1])
                t_x2b = Sh.op("vector", lambda e, b=bi: e.tensor_scalar(
                    out=xn2b[b][:], in0=xn[b][:], scalar1=negm2[b][:],
                    scalar2=rstd2[b][:], op0=ALU.add, op1=ALU.mult),
                    [(ACT, t_e2[1]), (VE, t_nm2[1]), prev_pe_xn2T[bi]], inc=VE)
                rstd_g[bi] = (VE, t_x2b[1])
                flush_vevac()

                def ev_xnT(half, pe_tok, b=bi):
                    return Sh.op("scalar", lambda e, half=half, b=b:
                                 e.copy(
                                     out=xnT_t[b][:, half * 1024:(half + 1) * 1024],
                                     in_=tr_ps[:, half * 1024:(half + 1) * 1024]),
                                 [pe_tok, xnT_free[b]], inc=ACT)

                t_xnT_ev, pe_tok = transpose16(
                    tr_ps, tr_state, xnb[bi][:], [(VE, t_xnb[1])], ev_xnT)
                prev_pe_xnT[bi] = pe_tok
                tsp = dma("scalar", SPX[bi], xnT_sp[T], xnT_t[bi][:],
                          waits=[(ACT, t_xnT_ev[1])])
                t_xnT_sp[T] = tsp
                xnT_free[bi] = (SPX[bi], tsp[1])

                mm = None
                for ec in range(NCH):
                    w = []
                    if ec == 0:
                        w = [vps_free, (ACT, t_xnT_ev[1]), T_WV[0]]
                    elif ec == NCH // 2:
                        w = [T_WV[1]]
                    for fc in range(4):
                        mm = Sh.op("tensor", lambda e, ec=ec, fc=fc, b=bi:
                                   e.matmul(v_ps[:, fc * 512:(fc + 1) * 512],
                                            xnT_t[b][:, ec * 128:(ec + 1) * 128],
                                            wv_sb[:, ec * E + fc * 512:
                                                  ec * E + (fc + 1) * 512],
                                            start=(ec == 0), stop=(ec == NCH - 1)),
                                   w if fc == 0 else [],
                                   inc=PEm if ec == NCH - 1 else None)
                pend_vevac.append((T, (PEm, mm[1])))

                def ev_xn2T(half, pe_tok, b=bi):
                    last = None
                    for j in range(8):
                        c = half * 8 + j
                        w = [pe_tok, xn2T_free[b], T_CONST]
                        last = Sh.op("scalar", lambda e, c=c, j=j, half=half, b=b:
                                     e.activation(
                                         out=xn2T_t[b][:, c * 128:(c + 1) * 128],
                                         in_=tr_ps[:, half * 1024 + j * 128:
                                                   half * 1024 + (j + 1) * 128],
                                         func=AF.Identity,
                                         scale=lnw_c[:, c:c + 1],
                                         bias=lnb_c[:, c:c + 1]),
                                     w, inc=ACT if j == 7 else None)
                    return last

                t_xn2T_ev, pe_tok = transpose16(
                    tr_ps, tr_state, xn2b[bi][:], [(VE, t_x2b[1])], ev_xn2T)
                prev_pe_xn2T[bi] = pe_tok
                tsp = dma("scalar", SP2[bi], xn2T_sp[T], xn2T_t[bi][:],
                          waits=[(ACT, t_xn2T_ev[1])])
                t_xn2T_sp[T] = tsp
                xn2T_free[bi] = (SP2[bi], tsp[1])

            flush_vevac()

        def qk_phase(which):
            col0 = 0 if which == "q" else E
            sp = qT_sp if which == "q" else kT_sp
            spsem = SPQ if which == "q" else SPK
            with ExitStack() as pp:
                ee = pp.enter_context
                xnt_s = [ee(nc.sbuf_tensor(f"{which}_xnt{i}", [P, E], BF16))
                         for i in range(2)]
                w_sb = ee(nc.sbuf_tensor(f"{which}_w_sb", [P, NCH * E], BF16))
                cos_b = ee(nc.sbuf_tensor(f"{which}_cos_b", [P, E], F32))
                sin_b = ee(nc.sbuf_tensor(f"{which}_sin_b", [P, E], F32))
                b_b = ee(nc.sbuf_tensor(f"{which}_b_b", [P, E], F32))
                qf_t = ee(nc.sbuf_tensor(f"{which}_qf", [P, E], F32))
                qf = [qf_t, qf_t]
                qa_t = ee(nc.sbuf_tensor(f"{which}_qa", [P, E], F32))
                qa = [qa_t, qa_t]
                qb_t = ee(nc.sbuf_tensor(f"{which}_qb", [P, E], F32))
                qb = [qb_t, qb_t]
                qbf = [ee(nc.sbuf_tensor(f"{which}_qbf{i}", [P, E], BF16))
                       for i in range(2)]
                ss = [ee(nc.sbuf_tensor(f"{which}_ss{i}", [P, 1], F32))
                      for i in range(2)]
                qsc = [ee(nc.sbuf_tensor(f"{which}_qsc{i}", [P, 1], F32))
                       for i in range(2)]
                qT_t = [ee(nc.sbuf_tensor(f"{which}_qT{i}", [P, E], BF16))
                        for i in range(2)]
                q_ps = ee(nc.psum_tensor(f"{which}_q_ps", [P, E], F32))
                tr_ps = ee(nc.psum_tensor(f"{which}_tr_ps", [P, 2048], BF16))
                bc_ps = ee(nc.psum_tensor(f"{which}_bc_ps", [P, 512], F32))

                guard = phase_guard()
                gidx = 1 if which == "q" else 2
                for c in range(NCH):
                    dma("gpsimd", wsem(gidx, c), w_sb[:, c * E:(c + 1) * E],
                        wq_view[c, :, col0:col0 + E],
                        waits=guard if c == 0 else ())
                T_W = [(WGRP[gidx], WGRP[gidx].n), (WGRPB[gidx], WGRPB[gidx].n)]
                bc0 = 0 if which == "q" else NCH
                t_bb = build_bcast(b_b, bqkv_c[:], bc0, bc_ps, extra_waits=guard)
                t_cb = build_bcast(cos_b, cos_c[:], 0, bc_ps)
                t_sb = build_bcast(sin_b, sin_c[:], 0, bc_ps)
                T_SPX_ALL = [(SPX[0], SPX[0].n), (SPX[1], SPX[1].n)]

                tr_state = {"free": [NONE_TOK, NONE_TOK]}
                xnt_free = [NONE_TOK, NONE_TOK]
                qT_free = [NONE_TOK, NONE_TOK]
                qps_free = NONE_TOK
                ssq_g = [NONE_TOK, NONE_TOK]
                qbf_tr = [NONE_TOK, NONE_TOK]

                def emit_transposes(T):
                    bi = T % 2

                    def ev_qT(half, pe_tok, b=bi):
                        return Sh.op("vector", lambda e, half=half, b=b:
                                     e.tensor_copy(
                                         out=qT_t[b][:, half * 1024:(half + 1) * 1024],
                                         in_=tr_ps[:, half * 1024:(half + 1) * 1024]),
                                     [pe_tok, qT_free[b]], inc=VE)

                    t_ev, pe_tok = transpose16(tr_ps, tr_state, qbf[bi][:],
                                               [(VE, chain_tok[T])], ev_qT)
                    qbf_tr[bi] = pe_tok
                    tsp = dma("scalar", spsem[bi], sp[T], qT_t[bi][:],
                              waits=[(VE, t_ev[1])])
                    qk_spill[which][T] = tsp
                    qT_free[bi] = (spsem[bi], tsp[1])

                chain_tok = {}
                for T in range(NT):
                    bi = T % 2
                    t_x = dma("sync", SQX[bi], xnt_s[bi][:], xnT_sp[T],
                              waits=[xnt_free[bi]] + T_SPX_ALL)
                    TX = (SQX[bi], t_x[1])
                    mm = None
                    for ec in range(NCH):
                        w = [TX]
                        if ec == 0:
                            w += [qps_free, T_W[0]]
                        elif ec == NCH // 2:
                            w += [T_W[1]]
                        for fc in range(4):
                            mm = Sh.op("tensor", lambda e, ec=ec, fc=fc, b=bi:
                                       e.matmul(q_ps[:, fc * 512:(fc + 1) * 512],
                                                xnt_s[b][:, ec * 128:(ec + 1) * 128],
                                                w_sb[:, ec * E + fc * 512:
                                                     ec * E + (fc + 1) * 512],
                                                start=(ec == 0),
                                                stop=(ec == NCH - 1)),
                                       w if fc == 0 else [],
                                       inc=PEm if ec == NCH - 1 else None)
                    t_mm = (PEm, mm[1])
                    xnt_free[bi] = t_mm
                    if T > 0:
                        emit_transposes(T - 1)
                    evs_last = None
                    for fc in range(4):
                        evs_last = Sh.op("vector", lambda e, fc=fc, b=bi:
                                         e.tensor_tensor(
                                             out=qf[b][:, fc * 512:(fc + 1) * 512],
                                             in0=q_ps[:, fc * 512:(fc + 1) * 512],
                                             in1=b_b[:, fc * 512:(fc + 1) * 512],
                                             op=ALU.add),
                                         [t_mm, (VE, t_bb[1]),
                                          ssq_g[bi] if fc == 0 else NONE_TOK],
                                         inc=VE)
                    qps_free = (VE, evs_last[1])
                    t_a = Sh.op("vector", lambda e, b=bi: e.tensor_tensor(
                        out=qa[b][:], in0=qf[b][:], in1=cos_b[:], op=ALU.mult),
                        [(VE, t_sb[1]), ssq_g[0], ssq_g[1]], inc=VE)
                    t_b1 = Sh.op("vector", lambda e, b=bi: e.tensor_tensor(
                        out=qb[b][:, 0:1024], in0=qf[b][:, 1:E:2],
                        in1=sin_b[:, 0:1024], op=ALU.mult),
                        [ssq_g[0], ssq_g[1]], inc=VE)
                    t_b2 = Sh.op("vector", lambda e, b=bi: e.tensor_tensor(
                        out=qb[b][:, 1024:E], in0=qf[b][:, 0:E:2],
                        in1=sin_b[:, 1024:E], op=ALU.mult), [], inc=VE)
                    t_c1 = Sh.op("vector", lambda e, b=bi: e.tensor_tensor(
                        out=qa[b][:, 0:1024], in0=qa[b][:, 0:1024],
                        in1=qb[b][:, 0:1024], op=ALU.subtract),
                        [(VE, t_b1[1])], inc=VE)
                    t_c2 = Sh.op("vector", lambda e, b=bi: e.tensor_tensor(
                        out=qa[b][:, 1024:E], in0=qa[b][:, 1024:E],
                        in1=qb[b][:, 1024:E], op=ALU.add),
                        [(VE, t_b2[1])], inc=VE)
                    t_ssq = Sh.op("scalar", lambda e, b=bi: e.activation(
                        out=qb[b][:], in_=qa[b][:], func=AF.Square,
                        accum_out=ss[b][:]),
                        [(VE, t_c1[1]), (VE, t_c2[1])], inc=ACT)
                    ssq_g[bi] = (ACT, t_ssq[1])
                    t_l = Sh.op("scalar", lambda e, b=bi: e.activation(
                        out=ss[b][:], in_=ss[b][:], func=AF.Ln, bias=eps24b[:]),
                        [(ACT, t_ssq[1])], inc=ACT)
                    t_e = Sh.op("scalar", lambda e, b=bi: e.activation(
                        out=qsc[b][:], in_=ss[b][:], func=AF.Exp, scale=-0.5,
                        bias=lqkb[:]), [(ACT, t_l[1])], inc=ACT)
                    t_qbf = Sh.op("vector", lambda e, b=bi: e.tensor_scalar(
                        out=qbf[b][:], in0=qa[b][:], scalar1=qsc[b][:],
                        scalar2=None, op0=ALU.mult),
                        [(ACT, t_e[1]), qbf_tr[bi]], inc=VE)
                    chain_tok[T] = t_qbf[1]
                emit_transposes(NT - 1)

        qk_phase("q")
        qk_phase("k")

        with ExitStack() as p2:
            e2 = p2.enter_context
            kT_all = e2(nc.sbuf_tensor("kT_all", [P, NT * E], BF16))
            qt_s = [e2(nc.sbuf_tensor(f"qt_s{i}", [P, E], BF16)) for i in range(2)]
            expT = [e2(nc.sbuf_tensor(f"expT{i}", [P, 1024], BF16)) for i in range(2)]
            rinv = e2(nc.sbuf_tensor("rinv", [1, 1024], F32))
            rinvb = e2(nc.sbuf_tensor("rinvb", [P, 1024], F32))
            attnT_t = [e2(nc.sbuf_tensor(f"attnT_t{i}", [P, E], BF16))
                       for i in range(2)]
            wchunk = [e2(nc.sbuf_tensor(f"wchunk{i}", [P, E], F32))
                      for i in range(2)]
            cmax = e2(nc.sbuf_tensor("cmax", [P, 1], F32))
            sc_ps = [e2(nc.psum_tensor(f"sc_ps{i}", [P, 512], F32)) for i in range(2)]
            at_ps = e2(nc.psum_tensor("at_ps", [P, 1024], F32))
            rs_ps = e2(nc.psum_tensor("rs_ps", [1, 1024], F32))
            rb_ps = e2(nc.psum_tensor("rb_ps", [P, 1024], F32))

            guard = phase_guard()
            kt_view = kT_sp.rearrange("T p c -> p T c")
            KT4 = [WGRP[4], WGRPB[4], WO4[0], SG4[0]]
            T_KT4 = []
            for k4 in range(4):
                t_kt = Sh.op("sync", lambda e, k4=k4: e.dma_start(
                    out=kT_all[:].rearrange("p (T c) -> p T c", c=E)[:, k4 * 4:(k4 + 1) * 4, :],
                    in_=kt_view[:, k4 * 4:(k4 + 1) * 4, :]),
                    guard + [(SPK[0], SPK[0].n), (SPK[1], SPK[1].n)],
                    inc=KT4[k4], k=16)
                T_KT4.append((KT4[k4], t_kt[1]))
            T_SPQ_ALL = [(SPQ[0], SPQ[0].n), (SPQ[1], SPQ[1].n)]
            wf_view = wffnT_d.rearrange("(c p) f -> c p f", p=P)

            qt_free = [NONE_TOK, NONE_TOK]
            att_free = [NONE_TOK, NONE_TOK]
            atps_free = NONE_TOK
            exp_free = [NONE_TOK, NONE_TOK]
            sc_free = [NONE_TOK, NONE_TOK]
            rs_free = NONE_TOK
            rb_free = NONE_TOK
            rinv_free = NONE_TOK
            rinvb_free = NONE_TOK
            wck_free = [NONE_TOK, NONE_TOK]
            t_wmax = NONE_TOK
            pend_tail = [None]
            ev_done = [NONE_TOK, NONE_TOK]

            for T in range(NT):
                bi = T % 2
                t_q = dma("sync", SQT[bi], qt_s[bi][:], qT_sp[T],
                          waits=[qt_free[bi]] + T_SPQ_ALL)
                TQ = (SQT[bi], t_q[1])
                t_wc = dma("scalar", SWC[bi], wchunk[bi][:], wf_view[T],
                           waits=[wck_free[bi]] + (guard if T < 2 else []))
                t_cm = Sh.op("vector", lambda e, b=bi: e.tensor_reduce(
                    cmax[:], wchunk[b][:], axis=AX.X, op=ALU.max,
                    apply_absolute_value=True),
                    [(SWC[bi], t_wc[1]), t_wmax], inc=VE)
                if T == 0:
                    t_mxc = Sh.op("vector", lambda e: e.tensor_copy(
                        out=wmax[:], in_=cmax[:]), [(VE, t_cm[1])], inc=VE)
                else:
                    t_mxc = Sh.op("vector", lambda e: e.tensor_tensor(
                        out=wmax[:], in0=wmax[:], in1=cmax[:], op=ALU.max),
                        [(VE, t_cm[1])], inc=VE)
                t_wmax = (VE, t_mxc[1])
                wck_free[bi] = (VE, t_cm[1])

                last_av_tile = None
                last_ev = None
                for half in range(2):
                    last_av = None
                    last_rs = None
                    pend = None

                    def emit_av_rs(jb, ebi, t_ready, T=T, half=half):
                        nonlocal last_av, last_rs
                        for g in range(2):
                            for hl in range(4):
                                h = half * 8 + g * 4 + hl
                                loc = g * 4 + hl
                                w = []
                                if g == 0 and hl == 0:
                                    w = [t_ready]
                                    if jb == 0:
                                        w.append(atps_free)
                                last_av = Sh.op(
                                    "tensor",
                                    lambda e, h=h, loc=loc, jb=jb, ebi=ebi, T=T:
                                    e.matmul(at_ps[:, loc * 128:(loc + 1) * 128],
                                             v_all[:, jb * E + h * 128:
                                                   jb * E + (h + 1) * 128],
                                             expT[ebi][:, loc * 128:(loc + 1) * 128],
                                             start=(jb == 0 and loc % 4 == 0),
                                             stop=(jb == T)),
                                    w, inc=PEm if (g == 1 and hl == 3) else None)
                        for g in range(2):
                            w = [(VE, t_ones[1])]
                            if jb == 0 and g == 0:
                                w.append(rs_free)
                            last_rs = Sh.op(
                                "tensor", lambda e, g=g, ebi=ebi, jb=jb, T=T:
                                e.matmul(rs_ps[:, g * 512:(g + 1) * 512],
                                         onescol[:],
                                         expT[ebi][:, g * 512:(g + 1) * 512],
                                         start=(jb == 0), stop=(jb == T)),
                                w, inc=PEm if g == 1 else None)
                        exp_free[ebi] = (PEm, last_rs[1])

                    for jb in range(T + 1):
                        ebi = jb % 2
                        exp_last = None
                        for g in range(2):
                            mm_last = None
                            for hl in range(4):
                                h = half * 8 + g * 4 + hl
                                w = []
                                if hl == 0:
                                    w = [sc_free[g], TQ, T_KT4[jb // 4]]
                                mm_last = Sh.op(
                                    "tensor", lambda e, h=h, hl=hl, jb=jb, g=g, b=bi:
                                    e.matmul(sc_ps[g][:, hl * 128:(hl + 1) * 128],
                                             kT_all[:, jb * E + h * 128:
                                                    jb * E + (h + 1) * 128],
                                             qt_s[b][:, h * 128:(h + 1) * 128],
                                             start=True, stop=True),
                                    w, inc=PEm if hl == 3 else None)
                            exp_last = Sh.op(
                                "scalar", lambda e, g=g, ebi=ebi:
                                e.activation(out=expT[ebi][:, g * 512:(g + 1) * 512],
                                             in_=sc_ps[g][:, 0:512], func=AF.Exp),
                                [(PEm, mm_last[1]),
                                 exp_free[ebi] if g == 0 else NONE_TOK], inc=ACT)
                            sc_free[g] = (ACT, exp_last[1])
                        t_ready = (ACT, exp_last[1])
                        if jb == T:
                            mb = maskTb[:].unsqueeze(1).broadcast_to([P, 8, P])
                            ev = expT[ebi][:].rearrange("p (g c) -> p g c", c=P)
                            t_m = Sh.op("vector", lambda e, ev=ev, mb=mb:
                                        e.tensor_tensor(out=ev, in0=ev, in1=mb,
                                                        op=ALU.mult),
                                        [t_ready, T_CONST], inc=VE)
                            t_ready = (VE, t_m[1])
                        if pend_tail[0] is not None and jb == min(1, T):
                            pend_tail[0]()
                            pend_tail[0] = None
                        if pend is not None:
                            emit_av_rs(*pend)
                        pend = (jb, ebi, t_ready)
                    emit_av_rs(*pend)

                    t_ri = Sh.op("vector", lambda e: e.reciprocal(rinv[:], rs_ps[:]),
                                 [(PEm, last_rs[1]), rinv_free], inc=VE)
                    rs_free = (VE, t_ri[1])

                    def make_tail(t_ri=t_ri, last_av=last_av, half=half, bi=bi, T=T):
                        def tail():
                            nonlocal rb_free, rinv_free, rinvb_free, atps_free
                            nonlocal att_free, qt_free, ev_done
                            t_rb_last = None
                            for g in range(2):
                                w = [(VE, t_ri[1]), (VE, t_onesr[1])]
                                if g == 0:
                                    w.append(rb_free)
                                t_rb_last = Sh.op(
                                    "tensor", lambda e, g=g:
                                    e.matmul(rb_ps[:, g * 512:(g + 1) * 512],
                                             onesrow[:],
                                             rinv[:, g * 512:(g + 1) * 512],
                                             start=True, stop=True),
                                    w, inc=PEm)
                            t_rbe = Sh.op("vector", lambda e: e.tensor_copy(
                                out=rinvb[:], in_=rb_ps[:]),
                                [(PEm, t_rb_last[1]), rinvb_free], inc=VE)
                            rb_free = (VE, t_rbe[1])
                            rinv_free = (VE, t_rbe[1])
                            ev_last = None
                            for g in range(2):
                                ev_last = Sh.op(
                                    "vector", lambda e, g=g, half=half, b=bi:
                                    e.tensor_tensor(
                                        out=attnT_t[b][:, half * 1024 + g * 512:
                                                       half * 1024 + (g + 1) * 512],
                                        in0=at_ps[:, g * 512:(g + 1) * 512],
                                        in1=rinvb[:, g * 512:(g + 1) * 512],
                                        op=ALU.mult),
                                    [(PEm, last_av[1]), (VE, t_rbe[1]),
                                     att_free[bi] if g == 0 else NONE_TOK],
                                    inc=VE)
                            rinvb_free = (VE, ev_last[1])
                            atps_free = (VE, ev_last[1])
                            ev_done[bi] = (VE, ev_last[1])
                            qt_free[bi] = (PEm, last_av[1])
                            if half == 1:
                                tsp = dma("scalar", SPA[bi], attnT_sp[T],
                                          attnT_t[bi][:], waits=[ev_done[bi]])
                                t_attnT_sp[T] = tsp
                                att_free[bi] = (SPA[bi], tsp[1])
                        return tail

                    pend_tail[0] = make_tail()
            if pend_tail[0] is not None:
                pend_tail[0]()
                pend_tail[0] = None

        vsc.close()

        with ExitStack() as p4:
            e4 = p4.enter_context
            wo_sb = e4(nc.sbuf_tensor("wo_sb", [P, NCH * E], BF16))
            sg_sb = e4(nc.sbuf_tensor("sg_sb", [P, NCH * E], BF16))
            bout_b = e4(nc.sbuf_tensor("bout_b", [P, E], F32))
            gamS_b = e4(nc.sbuf_tensor("gamS_b", [P, E], F32))
            beta_b = e4(nc.sbuf_tensor("beta_b", [P, E], F32))

            guard = phase_guard()
            wo_view = woutT_d.rearrange("(c p) f -> c p f", p=P)
            for c in range(NCH):
                dma("gpsimd", WO4[c // 4], wo_sb[:, c * E:(c + 1) * E], wo_view[c],
                    waits=guard if c == 0 else ())
            T_WO = [(WO4[k], WO4[k].n) for k in range(4)]
            for c in range(NCH):
                dma("gpsimd", SG4[c // 4], sg_sb[:, c * E:(c + 1) * E], wf_view[c],
                    waits=guard if c == 0 else ())
            T_SG = [(SG4[k], SG4[k].n) for k in range(4)]
            t_sign = None
            sign_q = [None] * 4
            for c in range(NCH):
                t_sign = Sh.op("scalar", lambda e, c=c: e.activation(
                    out=sg_sb[:, c * E:(c + 1) * E],
                    in_=sg_sb[:, c * E:(c + 1) * E], func=AF.Sign),
                    [T_SG[c // 4]], inc=ACT)
                if c % 4 == 3:
                    sign_q[c // 4] = (ACT, t_sign[1])
            sign_done = (ACT, t_sign[1])

            with ExitStack() as p4a:
                e4a = p4a.enter_context
                mpad = e4a(nc.sbuf_tensor("mpad", [P, P], F32))
                mrow = e4a(nc.sbuf_tensor("mrow", [1, P], F32))
                msc = e4a(nc.sbuf_tensor("msc", [1, 1], F32))
                gs_cols = e4a(nc.sbuf_tensor("gs_cols", [P, NCH], F32))
                msc_b = e4a(nc.sbuf_tensor("msc_b", [P, 1], F32))
                mt_ps = e4a(nc.psum_tensor("mt_ps", [P, P], F32))
                bc_ps4 = e4a(nc.psum_tensor("bc_ps4", [P, 512], F32))

                t_mz = Sh.op("vector", lambda e: e.memset(mpad[:], 0.0),
                             guard, inc=VE)
                t_mp = Sh.op("vector", lambda e: e.tensor_copy(
                    out=mpad[:, 0:1], in_=wmax[:]),
                    [t_wmax, (VE, t_mz[1])], inc=VE)
                t_tr = Sh.op("tensor", lambda e: e.transpose(mt_ps[:], mpad[:],
                                                             identf[:]),
                             [(VE, t_mp[1]), (VE, t_idf[1])] + guard, inc=PEm)
                t_mr = Sh.op("vector", lambda e: e.tensor_copy(
                    out=mrow[:], in_=mt_ps[0:1, :]), [(PEm, t_tr[1])], inc=VE)
                t_ms = Sh.op("vector", lambda e: e.tensor_reduce(
                    msc[:], mrow[:], axis=AX.X, op=ALU.max),
                    [(VE, t_mr[1])], inc=VE)
                t_msb_pe = Sh.op("tensor", lambda e: e.matmul(
                    mt_ps[:, 0:1], onesrow[:], msc[:], start=True, stop=True),
                    [(VE, t_ms[1]), (VE, t_onesr[1])], inc=PEm)
                t_msb = Sh.op("vector", lambda e: e.tensor_copy(
                    out=msc_b[:], in_=mt_ps[:, 0:1]), [(PEm, t_msb_pe[1])], inc=VE)
                t_gs = Sh.op("vector", lambda e: e.tensor_scalar(
                    out=gs_cols[:], in0=gamma_c[:], scalar1=msc_b[:], scalar2=None,
                    op0=ALU.mult), [(VE, t_msb[1]), T_CONST], inc=VE)
                t_gsb = build_bcast(gamS_b, gs_cols[:], 0, bc_ps4,
                                    extra_waits=[(VE, t_gs[1])] + guard)
                t_bb4 = build_bcast(beta_b, beta_c[:], 0, bc_ps4)
                t_bob = build_bcast(bout_b, bout_c[:], 0, bc_ps4)
            p4a_guard = phase_guard()

            at_s = [e4(nc.sbuf_tensor(f"at_s{i}", [P, E], BF16)) for i in range(2)]
            x2t_s = [e4(nc.sbuf_tensor(f"x2t_s{i}", [P, E], BF16)) for i in range(2)]
            xb4_t = e4(nc.sbuf_tensor("xb4_0", [P, E], F32))
            xb4 = [xb4_t, xb4_t]
            s_f_t = e4(nc.sbuf_tensor("s_f", [P, E], F32))
            s_f = [s_f_t, s_f_t]
            f_f_t = e4(nc.sbuf_tensor("f_f", [P, E], F32))
            f_f = [f_f_t, f_f_t]
            g_f = e4(nc.sbuf_tensor("g_f", [P, E], F32))
            s_ps = e4(nc.psum_tensor("s_ps", [P, E], F32))
            f_ps = e4(nc.psum_tensor("f_ps", [P, E], F32))

            T_SPA_ALL = [(SPA[0], SPA[0].n), (SPA[1], SPA[1].n)]
            T_SP2_ALL = [(SP2[0], SP2[0].n), (SP2[1], SP2[1].n)]
            at_free = [NONE_TOK, NONE_TOK]
            x2t_free = [NONE_TOK, NONE_TOK]
            xb4_free = NONE_TOK
            sps_free = list(p4a_guard[:3])
            fps_free = list(p4a_guard[:3])
            sf_free = NONE_TOK
            sfx_free = NONE_TOK
            ff_free = NONE_TOK
            ffx_free = NONE_TOK
            gf_free = NONE_TOK
            t_out_last = None
            for T in range(NT):
                bi = T % 2
                afb = at_free[bi] if isinstance(at_free[bi], list) else [at_free[bi]]
                t_at = dma("sync", SAT[bi], at_s[bi][:], attnT_sp[T],
                           waits=afb + T_SPA_ALL + p4a_guard)
                t_x2 = dma("sync", SX2[bi], x2t_s[bi][:], xn2T_sp[T],
                           waits=[x2t_free[bi]] + T_SP2_ALL)
                t_xb = dma("sync", SXB, xb4[bi][:], x_d[T * P:(T + 1) * P, :],
                           waits=[xb4_free])
                mm = None
                for ec in range(NCH):
                    w = [(SAT[bi], t_at[1])]
                    if ec == 0:
                        w += list(sps_free) + [T_WO[0]]
                    elif ec % 4 == 0:
                        w += [T_WO[ec // 4]]
                    for fc in range(4):
                        mm = Sh.op("tensor", lambda e, ec=ec, fc=fc, b=bi:
                                   e.matmul(s_ps[:, fc * 512:(fc + 1) * 512],
                                            at_s[b][:, ec * 128:(ec + 1) * 128],
                                            wo_sb[:, ec * E + fc * 512:
                                                  ec * E + (fc + 1) * 512],
                                            start=(ec == 0), stop=(ec == NCH - 1)),
                                   w if fc == 0 else [],
                                   inc=PEm if ec == NCH - 1 else None)
                t_smm = (PEm, mm[1])
                if debug:
                    td = dma("gpsimd", SDS, dbg["at"][T], at_s[bi][:],
                             waits=[(SAT[bi], t_at[1])])
                    at_free[bi] = [t_smm, (SDS, td[1])]
                else:
                    at_free[bi] = t_smm
                evs_last = None
                for fc in range(4):
                    evs_last = Sh.op("vector", lambda e, fc=fc, b=bi: e.tensor_tensor(
                        out=s_f[b][:, fc * 512:(fc + 1) * 512],
                        in0=s_ps[:, fc * 512:(fc + 1) * 512],
                        in1=bout_b[:, fc * 512:(fc + 1) * 512], op=ALU.add),
                        [t_smm, (VE, t_bob[1])] +
                        ([sf_free, sfx_free] if fc == 0 else []), inc=VE)
                t_sf = (VE, evs_last[1])
                if debug:
                    td = dma("gpsimd", SDS, dbg["s"][T * P:(T + 1) * P, :],
                             s_f[bi][:], waits=[t_sf])
                    sf_x = (SDS, td[1])
                else:
                    sf_x = NONE_TOK
                mm = None
                for ec in range(NCH):
                    w = [(SX2[bi], t_x2[1])]
                    if ec == 0:
                        w += list(fps_free) + [sign_q[0]]
                    elif ec % 4 == 0:
                        w += [sign_q[ec // 4]]
                    for fc in range(4):
                        mm = Sh.op("tensor", lambda e, ec=ec, fc=fc, b=bi:
                                   e.matmul(f_ps[:, fc * 512:(fc + 1) * 512],
                                            x2t_s[b][:, ec * 128:(ec + 1) * 128],
                                            sg_sb[:, ec * E + fc * 512:
                                                  ec * E + (fc + 1) * 512],
                                            start=(ec == 0), stop=(ec == NCH - 1)),
                                   w if fc == 0 else [],
                                   inc=PEm if ec == NCH - 1 else None)
                t_fmm = (PEm, mm[1])
                x2t_free[bi] = t_fmm
                evs_last = None
                for fc in range(4):
                    evs_last = Sh.op("vector", lambda e, fc=fc, b=bi: e.tensor_tensor(
                        out=f_f[b][:, fc * 512:(fc + 1) * 512],
                        in0=f_ps[:, fc * 512:(fc + 1) * 512],
                        in1=gamS_b[:, fc * 512:(fc + 1) * 512], op=ALU.mult),
                        [t_fmm, (VE, t_gsb[1])] +
                        ([ff_free, ffx_free] if fc == 0 else []), inc=VE)
                fps_free = [(VE, evs_last[1])]
                t_fb = Sh.op("vector", lambda e, b=bi: e.tensor_tensor(
                    out=f_f[b][:], in0=f_f[b][:], in1=beta_b[:], op=ALU.add),
                    [(VE, t_bb4[1])], inc=VE)
                if debug:
                    td = dma("gpsimd", SDF, dbg["f"][T * P:(T + 1) * P, :],
                             f_f[bi][:], waits=[(VE, t_fb[1])])
                    ff_x = (SDF, td[1])
                else:
                    ff_x = NONE_TOK
                t_g2 = Sh.op("scalar", lambda e, b=bi: e.activation(
                    out=g_f[:], in_=f_f[b][:], func=AF.Silu),
                    [(VE, t_fb[1]), gf_free], inc=ACT)
                ff_free = (ACT, t_g2[1])
                ffx_free = ff_x
                if debug:
                    td = dma("gpsimd", SDG, dbg["g"][T * P:(T + 1) * P, :],
                             g_f[:], waits=[(ACT, t_g2[1])])
                    gf_x = (SDG, td[1])
                else:
                    gf_x = NONE_TOK
                t_sg = Sh.op("vector", lambda e, b=bi: e.tensor_tensor(
                    out=g_f[:], in0=g_f[:], in1=s_f[b][:], op=ALU.mult),
                    [(ACT, t_g2[1]), t_sf, gf_x], inc=VE)
                t_o = Sh.op("vector", lambda e, b=bi: e.tensor_tensor(
                    out=s_f[b][:], in0=g_f[:], in1=xb4[b][:], op=ALU.add),
                    [(VE, t_sg[1]), (SXB, t_xb[1]), sf_x], inc=VE)
                gf_free = (VE, t_o[1])
                xb4_free = (VE, t_o[1])
                sps_free = [t_sf]
                t_st = dma("gpsimd", OUTS, out_d[T * P:(T + 1) * P, :], s_f[bi][:],
                           waits=[(VE, t_o[1])])
                sf_free = (OUTS, t_st[1])
                sfx_free = sf_x
                t_out_last = t_st

            if debug:
                fin = [(OUTS, 16 * NT), (VE, VE.n), (ACT, ACT.n), (PEm, PEm.n)]
                dma("gpsimd", SDS, dbg["sg_sb"], sg_sb[:], waits=fin)
                dma("gpsimd", SDS, dbg["wo_sb"], wo_sb[:], waits=fin)
                dma("gpsimd", SDS, dbg["bout_b"], bout_b[:], waits=fin)
                dma("gpsimd", SDS, dbg["gamS_b"], gamS_b[:], waits=fin)
                dma("gpsimd", SDS, dbg["beta_b"], beta_b[:], waits=fin)

            Sh.wait("gpsimd", (OUTS, 16 * NT))
            Sh.wait("sync", (OUTS, 16 * NT))

        blk = ctx.enter_context(nc.Block())
        Sh.replay(blk)

    return nc


_CONSTS = None


def _consts():
    global _CONSTS
    if _CONSTS is None:
        inv = 10000.0 ** (-(np.arange(0, E, 2, dtype=np.float64) / E))
        emb = np.repeat(inv, 2)
        sin = np.sin(S * emb).astype(np.float32).reshape(1, E)
        cos = np.cos(S * emb).astype(np.float32).reshape(1, E)
        ident = np.eye(P).astype(ml_dtypes.bfloat16)
        jj = np.arange(P).reshape(P, 1)
        ii = np.arange(P).reshape(1, P)
        maskT = (jj <= ii).astype(ml_dtypes.bfloat16)
        _CONSTS = (sin, cos, ident, maskT)
    return _CONSTS


def make_in_maps(L0_attn_x, L0_attn_Wqkv, L0_attn_bqkv, L0_attn_Wout, L0_attn_bout,
                 L0_ffn_weight, L0_ffn_gamma, L0_ffn_beta, L0_ffn_ln_weight,
                 L0_ffn_ln_bias):
    x = np.ascontiguousarray(np.asarray(L0_attn_x, dtype=np.float32))
    wqkvT = np.ascontiguousarray(np.asarray(L0_attn_Wqkv, np.float32).T)
    woutT = np.ascontiguousarray(np.asarray(L0_attn_Wout, np.float32).T)
    wffnT = np.ascontiguousarray(np.asarray(L0_ffn_weight, np.float32).T)

    def cols(a, n):
        return np.ascontiguousarray(np.asarray(a, np.float32).reshape(n, P).T)

    sin, cos, ident, maskT = _consts()
    shared = dict(wqkvT=wqkvT, woutT=woutT, wffnT=wffnT,
                  bqkv_c=cols(L0_attn_bqkv, 3 * NCH),
                  bout_c=cols(L0_attn_bout, NCH),
                  gamma_c=cols(L0_ffn_gamma, NCH),
                  beta_c=cols(L0_ffn_beta, NCH),
                  lnw_c=cols(L0_ffn_ln_weight, NCH),
                  lnb_c=cols(L0_ffn_ln_bias, NCH),
                  cos_c=cols(cos, NCH), sin_c=cols(sin, NCH),
                  identb=ident, maskTb=maskT)
    return [dict(x=np.ascontiguousarray(x[b]), **shared) for b in range(x.shape[0])]


def kernel(L0_attn_x, L0_attn_Wqkv, L0_attn_bqkv, L0_attn_Wout, L0_attn_bout,
           L0_ffn_weight, L0_ffn_gamma, L0_ffn_beta, L0_ffn_ln_weight,
           L0_ffn_ln_bias):
    in_maps = make_in_maps(L0_attn_x, L0_attn_Wqkv, L0_attn_bqkv, L0_attn_Wout,
                           L0_attn_bout, L0_ffn_weight, L0_ffn_gamma, L0_ffn_beta,
                           L0_ffn_ln_weight, L0_ffn_ln_bias)
    nc = build_nc()
    res = run_bass_kernel_spmd(nc, in_maps, core_ids=list(range(8)))
    return np.stack([np.asarray(res.results[b]["out"]) for b in range(8)], axis=0)
